# version 15
# baseline (speedup 1.0000x reference)
import contextlib

import numpy as np
import ml_dtypes

import concourse.bass as bass
import concourse.mybir as mybir
from concourse.bass_utils import run_bass_kernel_spmd

F32 = mybir.dt.float32
BF16 = mybir.dt.bfloat16
INT8 = mybir.dt.int8
AF = mybir.ActivationFunctionType
ALU = mybir.AluOpType

MAGIC = 12582912.0
BIGNEG = -1.0e9
EPS = 1e-5

T = 2048
NTT = 16
NCH = 16
D = 2048
NDC = 16
H = 32
P = 64
PROJ = 5152
NFC = 41
CONVC = 24
F2 = 4096
NF2C = 32
NQ = 4
TUPH = 256 * 4096
TDNH = 512 * 2048

ENGINES = ("sync", "scalar", "tensor", "vector", "gpsimd")
_FAKE_CC = False
_SKIP_CC = False
_S6_LIMIT = 9


class Tape:

    def __init__(self):
        self.items = {e: [] for e in ENGINES}
        self.counts = {}
        self.last_wait = {e: {} for e in ENGINES}
        self.handles = {}
        self._names = []
        self.pending_barrier = {e: [] for e in ENGINES}
        for s in ("sX", "sPE", "sV", "sA", "sP", "sCC"):
            self.sem(s)

    def sem(self, name):
        if name not in self.counts:
            self.counts[name] = 0
            self._names.append(name)
        return name

    def stage_barrier(self):
        bl = [(n, v) for n, v in self.counts.items() if v > 0]
        self.pending_barrier = {e: list(bl) for e in ENGINES}

    ENG_SEM = {"scalar": "sA", "vector": "sV", "gpsimd": "sP"}

    def emit(self, eng, fn, waits=(), inc=None, inc_amt=1, dma=None):
        if dma is None:
            dma = (inc_amt == 16)
        waits = list(waits)
        if self.pending_barrier.get(eng):
            waits = self.pending_barrier[eng] + waits
            self.pending_barrier[eng] = []
        if inc == "sX":
            waits.append(("sX", self.counts.get("sX", 0)))
        if not dma and eng in self.ENG_SEM:
            own = self.ENG_SEM[eng]
            waits.append((own, self.counts.get(own, 0)))
            if inc is None:
                inc = own
                inc_amt = 1
        lw = self.last_wait[eng]
        rw = []
        for s, v in waits:
            if v > lw.get(s, 0):
                lw[s] = v
                rw.append((s, v))
        if inc is not None:
            self.sem(inc)
            self.counts[inc] += inc_amt

        def run(e, _fn=fn, _w=rw, _inc=inc, _amt=inc_amt):
            for s, v in _w:
                e.wait_ge(self.handles[s], v)
            ins = _fn(e)
            if _inc is not None:
                ins.then_inc(self.handles[_inc], _amt)
        self.items[eng].append(run)
        return self.counts[inc] if inc is not None else None

    def now(self, sem):
        return self.counts.get(sem, 0)


class MK:

    def __init__(self):
        self.nc = bass.Bass("TRN2", target_bir_lowering=False)
        self.tape = Tape()
        self.stack = contextlib.ExitStack()
        self._declare_dram()
        self._alloc_sbuf()

    def _declare_dram(self):
        nc = self.nc

        def din(name, shape, dt=F32):
            return nc.dram_tensor(name, shape, dt, kind="ExternalInput").ap()

        self.x_in = din("x_in", [T + 1, D])
        self.wT = din("wT", [D, PROJ])
        self.woT = din("woT", [D, D])
        if _FAKE_CC:
            self.wupT = din("wupT", [D, F2])
            self.wdnT = din("wdnT", [F2, D])
        else:
            self.wupP = din("wupP", [256, F2])
            self.wdnP = din("wdnP", [512, D])
        self.w0c = din("w0c", [128, CONVC])
        self.w1c = din("w1c", [128, CONVC])
        self.wbc = din("wbc", [128, CONVC])
        self.dtb = din("dtb", [H, 1])
        self.alog = din("alog", [H, 1])
        self.gnormc = din("gnormc", [128, 1])
        self.ltbigT = din("ltbigT", [128, H * 128])
        self.h128 = din("h128", [128, 128], BF16)
        self.ident = din("ident", [128, 128])
        self.identb = din("identb", [128, 128], BF16)
        self.sel8 = din("sel8", [1, 8])
        self.em8 = din("em8", [1, 64])
        self.om8 = din("om8", [1, 64])
        self.y_out = nc.dram_tensor("y_out", [T, D], F32,
                                    kind="ExternalOutput").ap()

        def dscr(name, shape, dt=F32):
            return nc.dram_tensor(name, shape, dt, kind="Internal").ap()

        self.xnD = dscr("xnD", [NDC, 128, T], BF16)
        self.qD = dscr("qD", [16, 128, T], BF16)
        self.sggD = dscr("sggD", [16, 128, T], BF16)
        self.ydD = dscr("ydD", [NCH, 128, H * P], BF16)
        self.rowD = dscr("rowD", [NCH, H * 128])
        self.ddT = dscr("ddT", [NCH, 2048])
        self.bndD = dscr("bndD", [D], BF16)
        self.drowD = dscr("drowD", [32])
        self.mrowD = dscr("mrowD", [8, 2048])
        self.ternPinU = dscr("ternPinU", [TUPH], BF16)
        self.ternPoutU = dscr("ternPoutU", [8 * TUPH], BF16)
        self.ternPinD = dscr("ternPinD", [TDNH], BF16)
        self.ternPoutD = dscr("ternPoutD", [8 * TDNH], BF16)
        self.ccSclIn = dscr("ccSclIn", [2])
        self.ccSclOut = dscr("ccSclOut", [16])
        self.sclbD = dscr("sclbD", [256])
        self.swD = dscr("swD", [2])
        self.rawD = dscr("rawD", [NDC, 128, T], BF16)
        self.betaD = dscr("betaD", [T])
        self.ccinD = dscr("ccinD", [128, 1056])
        self.ccoutD = dscr("ccoutD", [8 * 128, 1056])

    def _alloc_sbuf(self):
        nc, st = self.nc, self.stack

        def sbuf(name, shape, dt):
            return st.enter_context(nc.sbuf_tensor(name, shape, dt))

        self.A1 = sbuf("A1", [128, 32768], BF16)
        self.A2 = sbuf("A2", [128, 32768], BF16)
        self.A3 = sbuf("A3", [128, 12288], BF16)
        self.A4 = sbuf("A4", [128, 16384], BF16)

        self.stacked_tm = sbuf("stacked_tm", [128, NCH, 96], F32)
        self.w0_sb = sbuf("w0_sb", [128, CONVC], F32)
        self.w1_sb = sbuf("w1_sb", [128, CONVC], F32)
        self.wb_sb = sbuf("wb_sb", [128, CONVC], F32)
        self.gn_sb = sbuf("gn_sb", [128, 1], F32)
        self.dtb_sb = sbuf("dtb_sb", [H, 1], F32)
        self.alog_sb = sbuf("alog_sb", [H, 1], F32)
        self.acol_sb = sbuf("acol_sb", [H, 1], F32)
        self.h128_sb = sbuf("h128_sb", [128, 128], BF16)
        self.id_sb = sbuf("id_sb", [128, 128], F32)
        self.idb_sb = sbuf("idb_sb", [128, 128], BF16)
        self.bndf_sb = sbuf("bndf_sb", [128, NDC], BF16)
        self.bndp_sb = sbuf("bndp_sb", [128, CONVC], F32)
        self.alpha_sb = sbuf("alpha_sb", [128, NTT], F32)
        self.sw_sb = sbuf("sw_sb", [128, 2], F32)
        self.inv_sb = sbuf("inv_sb", [128, 2], F32)
        self.sel_sb = sbuf("sel_sb", [1, 8], F32)
        self.em_sb = sbuf("em_sb", [1, 64], F32)
        self.om_sb = sbuf("om_sb", [1, 64], F32)
        self.dall_sb = sbuf("dall_sb", [1, 256], F32)
        self.mr_sb = sbuf("mr_sb", [1, 64], F32)
        self.mrow_sb = sbuf("mrow_sb", [1, 2048], F32)
        self.smallf = sbuf("smallf", [128, 16], F32)
        self.drow_sb = sbuf("drow_sb", [1, 32], F32)
        self.drep_sb = sbuf("drep_sb", [128, 32], F32)
        self.csts = sbuf("csts", [128, 4], F32)
        self.smallk = sbuf("smallk", [128, 3072], BF16)

        def psum(name, shape):
            return st.enter_context(nc.psum_tensor(name, shape, F32))

        self.pb = [psum(f"pb{i}", [128, 512]) for i in range(6)]
        self.pst = psum("pst", [128, 1024])

    @staticmethod
    def qhalf(h):
        return (h >> 2) & 1

    @staticmethod
    def qslot(h):
        return ((h >> 3) << 2) + (h & 3)

    def carve(self, arena, byte_off, shape, dt):
        esz = 4 if dt == F32 else 2
        ap = arena.ap().bitcast(dt) if dt != BF16 else arena.ap()
        n = 1
        for s in shape[1:]:
            n *= s
        assert byte_off % esz == 0
        start = byte_off // esz
        v = ap[0:shape[0], start:start + n]
        if len(shape) == 3:
            v = v.rearrange("p (a b) -> p a b", a=shape[1], b=shape[2])
        elif len(shape) == 4:
            v = v.rearrange("p (a b c) -> p a b c", a=shape[1], b=shape[2],
                            c=shape[3])
        return v

    def s0(self):
        E, t = self.tape.emit, self.tape

        consts = [(self.w0_sb, self.w0c), (self.w1_sb, self.w1c),
                  (self.wb_sb, self.wbc), (self.gn_sb, self.gnormc),
                  (self.dtb_sb, self.dtb), (self.alog_sb, self.alog),
                  (self.h128_sb, self.h128), (self.id_sb, self.ident),
                  (self.idb_sb, self.identb), (self.sel_sb, self.sel8),
                  (self.em_sb, self.em8), (self.om_sb, self.om8)]
        for dst, src in consts:
            E("sync", lambda e, d=dst, s=src: e.dma_start(out=d[:], in_=s),
              inc="cst", inc_amt=16)
        self.cst_tok = ("cst", t.now("cst"))

        E("vector", lambda e: e.memset(self.csts[:, 0:1], EPS), inc="sV")
        E("vector", lambda e: e.memset(self.csts[:, 1:2], -0.5), inc="sV")
        E("vector", lambda e: e.memset(self.csts[:, 2:3], MAGIC), inc="sV")
        E("vector", lambda e: e.memset(self.csts[:, 3:4], -MAGIC), inc="sV")

        xcur = [self.carve(self.A2, 40960, [128, D], F32),
                self.carve(self.A2, 49152, [128, D], F32)]
        scr0 = self.carve(self.A2, 57344, [128, D], F32)
        xn_tm = [self.carve(self.A2, 8192, [128, D], BF16),
                 self.carve(self.A2, 12288, [128, D], BF16)]
        self.xn_fm = self.carve(self.A1, 0, [128, NDC, T], BF16)
        stat = self.carve(self.A3, 10240, [128, 8], F32)

        def ln_body(part, xv, xnv, first_waits):
            E("scalar", lambda e: e.activation(
                scr0[0:part, :], xv, AF.Identity,
                accum_out=stat[0:part, 0:1]), waits=first_waits, inc="sA")
            E("scalar", lambda e: e.activation(
                scr0[0:part, :], xv, AF.Square,
                accum_out=stat[0:part, 1:2]), inc="sA")
            aw = t.now("sA")
            E("vector", lambda e: e.tensor_scalar(
                stat[0:part, 2:3], stat[0:part, 0:1], 1.0 / D, None,
                ALU.mult), waits=[("sA", aw)], inc="sV")
            E("vector", lambda e: e.tensor_mul(
                stat[0:part, 6:7], stat[0:part, 2:3], stat[0:part, 2:3]),
                inc="sV")
            E("vector", lambda e: e.tensor_scalar(
                stat[0:part, 3:4], stat[0:part, 1:2], 1.0 / D, None,
                ALU.mult), inc="sV")
            E("vector", lambda e: e.tensor_sub(
                stat[0:part, 3:4], stat[0:part, 3:4], stat[0:part, 6:7]),
                inc="sV")
            vw = t.now("sV")
            E("scalar", lambda e: e.activation(
                stat[0:part, 5:6], stat[0:part, 3:4], AF.Ln,
                bias=self.csts[0:part, 0:1]),
                waits=[("sV", vw)], inc="sA")
            E("scalar", lambda e: e.activation(
                stat[0:part, 4:5], stat[0:part, 5:6], AF.Exp,
                scale=self.csts[0:part, 1:2]),
                inc="sA")
            aw2 = t.now("sA")
            return E("vector", lambda e: e.tensor_scalar(
                xnv, xv, stat[0:part, 2:3], stat[0:part, 4:5],
                ALU.subtract, ALU.mult), waits=[("sA", aw2)], inc="sV")

        self._ln_body = ln_body
        xbnd = self.carve(self.A3, 0, [1, D], F32)
        xbnd_n = self.carve(self.A3, 8192, [1, D], BF16)
        E("sync", lambda e: e.dma_start(out=xbnd, in_=self.x_in[0:1, :]),
          inc="bxl", inc_amt=16)
        vb = ln_body(1, xbnd, xbnd_n, [("bxl", 16)])
        E("sync", lambda e: e.dma_start(out=self.bndD[:], in_=xbnd_n),
          waits=[("sV", vb)], inc="bnw", inc_amt=16)
        bnd16 = self.carve(self.A3, 4096, [16, 128], BF16)
        E("sync", lambda e: e.dma_start(
            out=bnd16, in_=self.bndD.rearrange("(a b) -> a b", b=128)),
            waits=[("bnw", 16)], inc="bnr", inc_amt=16)
        pt = self.pb[0].ap().bitcast(BF16)[:, 0:16]
        E("tensor", lambda e: e.transpose(
            pt, bnd16, self.idb_sb[0:16, 0:16]),
            waits=[("bnr", 16), self.cst_tok], inc="sPE")
        pw = t.now("sPE")
        E("vector", lambda e: e.tensor_copy(self.bndf_sb[:], pt),
            waits=[("sPE", pw)], inc="sV")
        self.bndf_ready = t.now("sV")

        xsem = ["xld0", "xld1"]
        xn_done = {}
        tp_done = {}
        for tt in range(NTT):
            b = tt % 2
            w = []
            if tt >= 2:
                w = [("sV", xn_done[tt - 2]), ("sX", tp_done[tt - 2])]
            E("sync", lambda e, b=b, r=1 + tt * 128: e.dma_start(
                out=xcur[b], in_=self.x_in[r:r + 128, :]),
                waits=w, inc=xsem[b], inc_amt=16)
            qw = (xsem[b], t.now(xsem[b]))
            v = ln_body(128, xcur[b], xn_tm[b], [qw])
            xn_done[tt] = v
            tp_done[tt] = E("sync", lambda e, b=b, tt=tt:
                            e.dma_start_transpose(
                self.xn_fm[:, :, tt * 128:(tt + 1) * 128], xn_tm[b]),
                waits=[("sV", v)], inc="sX", inc_amt=16)
        self.xn_ready = t.now("sX")
        self.tp_done = tp_done
        self.xld_toks = [("xld0", t.now("xld0")), ("xld1", t.now("xld1"))]

        for c in range(NDC):
            E("sync", lambda e, c=c: e.dma_start(
                out=self.xnD[c], in_=self.xn_fm[:, c, :]),
                waits=[("sX", self.xn_ready)], inc="xnw", inc_amt=16)
        self.xnw_tok = ("xnw", t.now("xnw"))


    def s1(self):
        E, t = self.tape.emit, self.tape

        wtile = [self.carve(self.A2, 0, [128, NDC, 128], BF16),
                 self.carve(self.A2, 4096, [128, NDC, 128], BF16)]
        self.kfm = self.carve(self.A2, 8192, [128, 4, T], BF16)
        self.kT = self.carve(self.A2, 24576, [128, NCH, 512], BF16)
        self.vT = self.carve(self.A2, 40960, [128, NCH, 512], BF16)
        out_bf = [self.carve(self.A2, 57344, [128, 512], BF16),
                  self.carve(self.A2, 58368, [128, 512], BF16),
                  self.carve(self.A2, 59392, [128, 512], BF16),
                  self.carve(self.A2, 60416, [128, 512], BF16)]
        conv_t1 = [self.carve(self.A3, 0, [128, 512], F32),
                   self.carve(self.A3, 2048, [128, 512], F32)]
        conv_o = [self.carve(self.A3, 4096, [128, 512], F32),
                  self.carve(self.A3, 6144, [128, 512], F32)]
        self.dt_pre = self.carve(self.A1, 0, [32, T], F32)
        self._conv_state = {"t1_war": {0: 0, 1: 0}, "co_war": {0: 0, 1: 0},
                            "ob_war": [("sV", 0)] * 4}

        wT4 = self.wT.rearrange("(dc p) f -> dc p f", p=128)
        wsem = ["wt0", "wt1"]

        def load_wtile(buf, fc, ncols, waits):
            for h in range(2):
                E("gpsimd",
                  lambda e, b=buf, fc=fc, h=h, ncols=ncols: e.dma_start(
                      out=wtile[b][:, h * 8:(h + 1) * 8, 0:ncols],
                      in_=wT4[h * 8:(h + 1) * 8, :,
                              fc * 128:fc * 128 + ncols]
                      .transpose([1, 0, 2])),
                  waits=waits if h == 0 else (),
                  inc=wsem[buf], inc_amt=16)
            return (wsem[buf], t.now(wsem[buf]))

        pbnd = self.pb[0].ap()
        ld = load_wtile(0, 0, 128, [("sV", self.bndf_ready)])
        for fc in range(CONVC):
            nxt = None
            if fc + 1 < CONVC:
                nxt = load_wtile((fc + 1) % 2, fc + 1, 128,
                                 [("sPE", t.now("sPE"))])
            for dc in range(NDC):
                E("tensor", lambda e, b=fc % 2, dc=dc, fc=fc: e.matmul(
                    pbnd[:, fc:fc + 1], wtile[b][:, dc, :],
                    self.bndf_sb[:, dc:dc + 1],
                    start=(dc == 0), stop=(dc == NDC - 1)),
                    waits=[ld] if dc == 0 else (), inc="sPE")
            if nxt is not None:
                ld = nxt
        pe_bnd = t.now("sPE")
        E("vector", lambda e: e.tensor_copy(
            self.bndp_sb[:], pbnd[:, 0:CONVC]),
            waits=[("sPE", pe_bnd)], inc="sV")
        self.bnd_evac = t.now("sV")

        groups = [[self.pb[0].ap(), self.pb[1].ap(), self.pb[2].ap(),
                   self.pb[3].ap()],
                  [self.pb[4].ap(), self.pb[5].ap(),
                   self.pst.ap()[:, 0:512], self.pst.ap()[:, 512:1024]]]

        def bank_ap(g, i):
            return groups[g][i]

        grp_free = [[("sV", self.bnd_evac)], [()]]
        grp_free[1] = []
        ld = load_wtile(0, 0, 128, [("sV", self.bnd_evac)])

        for fc in range(NFC):
            g = fc % 2
            buf = fc % 2
            ncols = 128 if fc < 40 else 32
            nxt = None
            if fc + 1 < NFC:
                ncols2 = 128 if fc + 1 < 40 else 32
                nxt = load_wtile((fc + 1) % 2, fc + 1, ncols2,
                                 [("sPE", t.now("sPE"))])
            for tg in range(4):
                for dc in range(NDC):
                    w = []
                    if dc == 0:
                        w = list(grp_free[g])
                        if tg == 0:
                            w.append(ld)
                        if fc == 0:
                            w.append(("sX", self.tp_done[tg * 4 + 3]))
                    E("tensor", lambda e, b=buf, dc=dc, tg=tg, g=g,
                        nc_=ncols: e.matmul(
                            bank_ap(g, tg)[0:nc_, :], wtile[b][:, dc, 0:nc_],
                            self.xn_fm[:, dc, tg * 512:(tg + 1) * 512],
                            start=(dc == 0), stop=(dc == NDC - 1)),
                        waits=w, inc="sPE")
            mm_done = t.now("sPE")
            if nxt is not None:
                ld = nxt

            if fc < CONVC:
                ep = self._conv_epilogue(fc, g, bank_ap, mm_done,
                                         conv_t1, conv_o, out_bf)
            elif fc < 40:
                ep = self._gate_epilogue(fc, g, bank_ap, mm_done, out_bf)
            else:
                ep = self._dt_epilogue(g, bank_ap, mm_done)
            grp_free[g] = ep

        self.qdw_toks = [(f"qdw{i}", t.now(f"qdw{i}")) for i in range(4)]
        self.sgw_toks = [(f"sgw{i}", t.now(f"sgw{i}")) for i in range(4)]
        self.proj_done_v = t.now("sV")
        self.s4a()

    def _conv_epilogue(self, fc, g, bank_ap, mm_done, conv_t1, conv_o,
                       out_bf):
        E, t = self.tape.emit, self.tape
        cs = self._conv_state
        w0 = self.w0_sb.ap()[:, fc:fc + 1]
        w1 = self.w1_sb.ap()[:, fc:fc + 1]
        wb = self.wb_sb.ap()[:, fc:fc + 1]
        for tg in range(4):
            bk = bank_ap(g, tg)
            t1 = conv_t1[tg % 2]
            co = conv_o[tg % 2]
            E("scalar", lambda e, bk=bk, t1=t1, w1=w1: e.activation(
                t1, bk, AF.Identity, scale=w1),
                waits=[("sPE", mm_done), ("sV", cs["t1_war"][tg % 2]),
                       self.cst_tok], inc="sA")
            aw = t.now("sA")
            E("vector", lambda e, bk=bk, t1=t1, co=co, w0=w0:
                e.scalar_tensor_tensor(
                    co[:, 1:512], bk[:, 0:511], w0, t1[:, 1:512],
                    ALU.mult, ALU.add),
                waits=[("sA", aw), ("sA", cs["co_war"][tg % 2])], inc="sV")
            if tg == 0:
                prev = self.bndp_sb.ap()[:, fc:fc + 1]
            else:
                prev = bank_ap(g, tg - 1)[:, 511:512]
            E("vector", lambda e, prev=prev, t1=t1, co=co, w0=w0:
                e.scalar_tensor_tensor(
                    co[:, 0:1], prev, w0, t1[:, 0:1], ALU.mult, ALU.add),
                inc="sV")
            cs["t1_war"][tg % 2] = t.now("sV")
            vw = t.now("sV")
            ob = out_bf[tg]
            E("scalar", lambda e, co=co, ob=ob, wb=wb: e.activation(
                ob, co, AF.Silu, bias=wb),
                waits=[("sV", vw), cs["ob_war"][tg]], inc="sA")
            cs["co_war"][tg % 2] = t.now("sA")
            sil = t.now("sA")
            if fc < 4:
                tok = E("sync", lambda e, ob=ob, fc=fc, tg=tg:
                        e.dma_start_transpose(
                    self.vT[:, tg * 4:(tg + 1) * 4,
                            fc * 128:(fc + 1) * 128], ob),
                    waits=[("sA", sil)], inc="sX", inc_amt=16)
                cs["ob_war"][tg] = ("sX", tok)
            elif fc < 8:
                E("vector", lambda e, ob=ob, fc=fc, tg=tg: e.tensor_copy(
                    self.kfm[:, fc - 4, tg * 512:(tg + 1) * 512], ob),
                    waits=[("sA", sil)], inc="sV")
                tok = E("sync", lambda e, ob=ob, fc=fc, tg=tg:
                        e.dma_start_transpose(
                    self.kT[:, tg * 4:(tg + 1) * 4,
                            (fc - 4) * 128:(fc - 3) * 128], ob),
                    waits=[("sA", sil)], inc="sX", inc_amt=16)
                cs["ob_war"][tg] = ("sX", tok)
            else:
                tok = E("sync", lambda e, ob=ob, fc=fc, tg=tg: e.dma_start(
                    out=self.qD[fc - 8, :, tg * 512:(tg + 1) * 512], in_=ob),
                    waits=[("sA", sil)], inc=f"qdw{tg}", inc_amt=16)
                cs["ob_war"][tg] = (f"qdw{tg}", tok)
        return [("sV", t.now("sV")), ("sA", t.now("sA"))]

    def _gate_epilogue(self, fc, g, bank_ap, mm_done, out_bf):
        E, t = self.tape.emit, self.tape
        cs = self._conv_state
        gc = fc - CONVC
        for tg in range(4):
            bk = bank_ap(g, tg)
            ob = out_bf[tg]
            E("scalar", lambda e, bk=bk, ob=ob: e.activation(
                ob, bk, AF.Silu),
                waits=[("sPE", mm_done), cs["ob_war"][tg]], inc="sA")
            aw = t.now("sA")
            E("vector", lambda e, ob=ob: e.tensor_scalar(
                ob, ob, self.gn_sb.ap()[:, 0:1], None, ALU.mult),
                waits=[("sA", aw)], inc="sV")
            vw = t.now("sV")
            tok = E("sync", lambda e, ob=ob, gc=gc, tg=tg: e.dma_start(
                out=self.sggD[gc, :, tg * 512:(tg + 1) * 512], in_=ob),
                waits=[("sV", vw)], inc=f"sgw{tg}", inc_amt=16)
            cs["ob_war"][tg] = (f"sgw{tg}", tok)
        return [("sV", t.now("sV")), ("sA", t.now("sA"))]

    def _dt_epilogue(self, g, bank_ap, mm_done):
        E, t = self.tape.emit, self.tape
        for tg in range(4):
            bk = bank_ap(g, tg)
            E("vector", lambda e, bk=bk, tg=tg: e.tensor_copy(
                self.dt_pre[:, tg * 512:(tg + 1) * 512], bk[0:32, :]),
                waits=[("sPE", mm_done), self.xnw_tok], inc="sV")
        return [("sV", t.now("sV"))]

    def s15(self):
        E, t = self.tape.emit, self.tape
        u = self.carve(self.A1, 8192, [32, T], F32)
        absu = self.carve(self.A1, 16384, [32, T], F32)
        dt = self.carve(self.A1, 24576, [32, T], F32)
        lndt = self.carve(self.A1, 32768, [32, T], F32)
        self.Acs = self.carve(self.A1, 40960, [32, T], F32)
        stacked = self.carve(self.A1, 49152, [96, T], F32)
        ones32 = self.carve(self.A3, 0, [32, 128], F32)
        dfm = self.carve(self.A3, 2048, [32, NCH], F32)
        ca = self.carve(self.A3, 4096, [32, NCH], F32)
        dtm = self.carve(self.A3, 6144, [NCH, 32], F32)
        drows = self.carve(self.A3, 8192, [NCH, 2048], F32)

        pv = self.proj_done_v
        E("vector", lambda e: e.memset(ones32, 1.0),
          waits=[self.xnw_tok], inc="sV")
        E("vector", lambda e: e.tensor_scalar(
            u, self.dt_pre, self.dtb_sb.ap(), None, ALU.add),
            waits=[("sV", pv)], inc="sV")
        vw = t.now("sV")
        E("scalar", lambda e: e.activation(absu, u, AF.Abs),
          waits=[("sV", vw)], inc="sA")
        E("scalar", lambda e: e.activation(absu, absu, AF.Exp, scale=-1.0),
          inc="sA")
        E("scalar", lambda e: e.activation(absu, absu, AF.Ln, bias=1.0),
          inc="sA")
        aw = t.now("sA")
        E("vector", lambda e: e.tensor_scalar(
            dt, u, 0.0, None, ALU.max), inc="sV")
        E("vector", lambda e: e.tensor_add(dt, dt, absu),
          waits=[("sA", aw)], inc="sV")
        vw = t.now("sV")
        E("scalar", lambda e: e.activation(lndt, dt, AF.Ln),
          waits=[("sV", vw)], inc="sA")
        E("scalar", lambda e: e.activation(
            self.acol_sb[:], self.alog_sb[:], AF.Exp), inc="sA")
        aw = t.now("sA")
        E("vector", lambda e: e.tensor_scalar(
            u, dt, self.acol_sb.ap(), -1.0, ALU.mult, ALU.mult),
            waits=[("sA", aw)], inc="sV")
        vw = t.now("sV")
        for c in range(NCH):
            E("vector", lambda e, c=c: e.tensor_tensor_scan(
                self.Acs[:, c * 128:(c + 1) * 128],
                u[:, c * 128:(c + 1) * 128], ones32, 0.0, ALU.add, ALU.mult),
                waits=[("sV", vw)], inc="sV")
        acs_done = t.now("sV")
        E("vector", lambda e: e.tensor_sub(stacked[0:32, :], lndt, self.Acs),
          waits=[("sV", acs_done)], inc="sV")
        asum = self.Acs.rearrange("p (c l) -> p c l", c=NCH)[:, :, 127]
        E("scalar", lambda e: e.activation(dfm, asum, AF.Exp),
          waits=[("sV", acs_done)], inc="sA")
        fb_tmp = self.carve(self.A1, 57344, [32, T], F32)
        for c in range(NCH):
            E("scalar", lambda e, c=c: e.activation(
                fb_tmp[:, c * 128:(c + 1) * 128],
                self.Acs[:, c * 128:(c + 1) * 128], AF.Exp, scale=-1.0,
                bias=self.Acs[:, c * 128 + 127:c * 128 + 128]),
                waits=[("sV", acs_done)], inc="sA")
        aw = t.now("sA")
        E("vector", lambda e: e.tensor_mul(fb_tmp, fb_tmp, dt),
            waits=[("sA", aw)], inc="sV")
        E("vector", lambda e: e.tensor_copy(stacked[32:64, :], fb_tmp),
            inc="sV")
        E("scalar", lambda e: e.activation(
            stacked[64:96, :], self.Acs, AF.Exp),
            waits=[("sV", acs_done)], inc="sA")
        st_done = (t.now("sV"), t.now("sA"))
        self._dt = dt
        self._stacked = stacked

        E("vector", lambda e: e.tensor_tensor_scan(
            ca, dfm, ones32[:, 0:NCH], 1.0, ALU.mult, ALU.mult), inc="sV")
        caw = t.now("sV")
        ptD = self.pb[0].ap()[:, 0:32]
        E("tensor", lambda e: e.transpose(
            ptD[0:1, 0:32], ca[:, 15:16], self.id_sb[0:32, 0:32]),
            waits=[("sV", caw)], inc="sPE")
        pw = t.now("sPE")
        E("vector", lambda e: e.tensor_copy(self.drow_sb[:], ptD[0:1, 0:32]),
          waits=[("sPE", pw)], inc="sV")
        dr = t.now("sV")
        drow_copied = dr
        E("sync", lambda e: e.dma_start(
            out=self.drowD.rearrange("(o a) -> o a", o=1), in_=self.drow_sb[:]),
            waits=[("sV", dr)], inc="drw", inc_amt=16)
        E("sync", lambda e: e.dma_start(
            out=self.drep_sb[:],
            in_=self.drowD.rearrange("(o a) -> o a", o=1).broadcast_to([128, 32])),
            waits=[("drw", 16)], inc="drr", inc_amt=16)
        self.drep_ready = ("drr", 16)

        ptd = self.pb[0].ap()[:, 64:96]
        E("tensor", lambda e: e.transpose(
            ptd[0:NCH, 0:32], dfm, self.id_sb[0:32, 0:32]),
            waits=[("sV", drow_copied)], inc="sPE")
        pw = t.now("sPE")
        E("vector", lambda e: e.tensor_copy(dtm, ptd[0:NCH, 0:32]),
          waits=[("sPE", pw)], inc="sV")
        dw = t.now("sV")
        dview = dtm.rearrange("c (kh2 b2 j) -> c b2 kh2 j", kh2=4, b2=2)
        E("vector", lambda e: e.tensor_copy(
            drows.rearrange("c (b2 kh2 j p) -> c b2 kh2 j p",
                            b2=2, kh2=4, j=4),
            dview.unsqueeze(4).broadcast_to([NCH, 2, 4, 4, P])),
            waits=[("sV", dw)], inc="sV")
        dw2 = t.now("sV")
        E("sync", lambda e: e.dma_start(out=self.ddT, in_=drows),
          waits=[("sV", dw2)], inc="ddw", inc_amt=16)
        self.ddT_ready = ("ddw", 16)

        for c in range(NCH):
            E("sync", lambda e, c=c: e.dma_start(
                out=self.rowD[c].rearrange("(h l) -> h l", l=128),
                in_=self.Acs[:, c * 128:(c + 1) * 128]),
                waits=[("sV", acs_done)], inc="row", inc_amt=16)
        self.rowD_ready = ("row", t.now("row"))

        pstk = self.pb[1].ap()[:, 0:96]
        for c in range(NCH):
            E("tensor", lambda e, c=c: e.transpose(
                pstk, stacked[:, c * 128:(c + 1) * 128],
                self.id_sb[0:96, 0:96]),
                waits=[("sV", st_done[0]), ("sA", st_done[1]),
                       ("sV", t.now("sV"))], inc="sPE")
            pw = t.now("sPE")
            E("vector", lambda e, c=c: e.tensor_copy(
                self.stacked_tm[:, c, :], pstk), waits=[("sPE", pw)],
                inc="sV")
        self.stacked_ready = t.now("sV")

    def s2(self):
        E, t = self.tape.emit, self.tape
        t.stage_barrier()

        Rt = [self.carve(self.A1, 0, [128, 4096], F32),
              self.carve(self.A1, 16384, [128, 4096], F32)]
        Et = [self.carve(self.A1, 32768, [128, 4096], BF16),
              self.carve(self.A1, 40960, [128, 4096], BF16)]
        Mt = [self.carve(self.A1, 49152, [128, 2048], BF16),
              self.carve(self.A1, 53248, [128, 2048], BF16)]
        ydt = self.carve(self.A1, 57344, [128, 2048], BF16)
        qbuf = [self.carve(self.A3, 0, [128, 32, 128], BF16),
                self.carve(self.A3, 8192, [128, 32, 128], BF16)]
        dmul = [self.carve(self.A3, 16384, [128, 1024], F32),
                self.carve(self.A3, 20480, [128, 1024], F32)]
        ktl2 = [self.carve(self.smallk, 128 * i, [128, 64], BF16)
                for i in range(16)]
        ktl_war = {i: 0 for i in range(16)}
        R = self.carve(self.smallk, 2048, [128, 1024], F32)
        self.staging = self.carve(self.A4, 0, [128, NCH, 1024], BF16)

        kv_ready = ("sX", self.xn_ready and t.now("sX"))
        qdw = self.qdw_toks

        rt_tok = {}
        ra_tok = {}
        qs_tok = {}
        dm_tok = {}
        rt_war = {0: [], 1: []}
        q_war = {0: [], 1: []}
        dm_war = {0: [], 1: []}
        e_war = {0: [], 1: []}
        m_war = {0: [], 1: []}

        def prefetch(c):
            b = c % 2
            rt_tok[c] = E("sync", lambda e, b=b: e.dma_start(
                out=Rt[b], in_=self.ltbigT.broadcast_to([128, 4096])
                if False else self.ltbigT),
                waits=[self.cst_tok] + rt_war[b], inc=f"rt{b}", inc_amt=16)
            for hv in range(2):
                ra_tok[c] = E("gpsimd", lambda e, b=b, c=c, hv=hv:
                              e.dma_start(
                    out=Rt[b][:, hv * 2048:(hv + 1) * 2048],
                    in_=self.rowD[c, hv * 2048:(hv + 1) * 2048]
                    .rearrange("(o a) -> o a", o=1)
                    .broadcast_to([128, 2048]),
                    accum_op=ALU.add),
                    waits=[(f"rt{b}", rt_tok[c]), self.rowD_ready]
                    if hv == 0 else (),
                    inc=f"ra{b}", inc_amt=16, dma=True)
            qv = self.qD \
                .rearrange("c (two p) t -> (c two) p t", two=2) \
                .rearrange("(kh2 half j) p t -> half kh2 j p t",
                           kh2=4, half=2, j=4)
            for hf in range(2):
                for kh2 in range(4):
                    h0 = kh2 * 8 + hf * 4
                    tok = E("sync", lambda e, b=b, c=c, hf=hf, kh2=kh2,
                            h0=h0, qv=qv: e.dma_start(
                        out=qbuf[b][hf * 64:(hf + 1) * 64, h0:h0 + 4, :],
                        in_=qv[hf, kh2, :, :, c * 128:(c + 1) * 128]
                        .transpose([1, 0, 2])),
                        waits=(qdw + q_war[b] +
                               [("sV", qz_init)]) if (hf == 0 and kh2 == 0)
                        else (), inc=f"qs{b}", inc_amt=16)
            qs_tok[c] = tok
            if c > 0:
                dm_tok[c] = E("sync", lambda e, b=b, c=c: e.dma_start(
                    out=dmul[b], in_=self.ddT[c]
                    .rearrange("(b2 rest) -> b2 rest", b2=2)
                    .unsqueeze(1).broadcast_to([2, 64, 1024])),
                    waits=[self.ddT_ready] + dm_war[b], inc=f"dm{b}",
                    inc_amt=16)

        E("vector", lambda e: e.memset(qbuf[0], 0.0))
        qz_init = E("vector", lambda e: e.memset(qbuf[1], 0.0))
        prefetch(0)
        prefetch(1)
        pst_free = []
        yd_free = []
        scan_tok = 0

        for c in range(NCH):
            b = c % 2
            vtoks = {}
            etoks = {}
            for hf2 in range(2):
                eng = "vector" if hf2 == 0 else "gpsimd"
                sem = "sV" if hf2 == 0 else "sP"
                ctv = self.stacked_tm.ap()[:, c, hf2 * 16:(hf2 + 1) * 16] \
                    .unsqueeze(2).broadcast_to([128, 16, 128])
                rv_ = Rt[b].rearrange("p (h l) -> p h l", h=32) \
                    [:, hf2 * 16:(hf2 + 1) * 16, :]
                vtoks[hf2] = E(eng, lambda e, ctv=ctv, rv_=rv_: e.tensor_add(
                    rv_, rv_, ctv),
                    waits=[(f"ra{b}", ra_tok[c]),
                           ("sV", self.stacked_ready)])
                etoks[hf2] = E("scalar", lambda e, b=b, hf2=hf2:
                               e.activation(
                    Et[b][:, hf2 * 2048:(hf2 + 1) * 2048],
                    Rt[b][:, hf2 * 2048:(hf2 + 1) * 2048], AF.Exp),
                    waits=[(sem, vtoks[hf2])] +
                    (e_war[b] if hf2 == 0 else []))
            etok = etoks[1]
            rt_war[b] = [("sA", etok)]

            mm1_done = {}
            mm3_done = {}
            st_written = []
            for half in range(2):
                for hh in range(16):
                    h = half * 16 + hh
                    kvh = h >> 2
                    lhs = self.kfm[:, kvh >> 1, c * 128:(c + 1) * 128]
                    rhs = qbuf[b][:, h, :]
                    bank = self.pb[hh // 4].ap()
                    w = []
                    if hh == 0:
                        w = [(f"qs{b}", qs_tok[c]), kv_ready,
                             ("sV", t.now("sV"))] + (pst_free if half == 0
                                                     else [])
                        if half == 1:
                            w = [("sV", t.now("sV"))]
                    E("tensor", lambda e, lhs=lhs, rhs=rhs, bank=bank, hh=hh:
                        e.matmul(bank[:, (hh % 4) * 128:(hh % 4 + 1) * 128],
                                 lhs, rhs, start=True, stop=True),
                        waits=w, inc="sPE")
                mm1_done[half] = t.now("sPE")

                ktoks = {}
                ksem = {}
                for hh in range(16):
                    h = half * 16 + hh
                    kvh = h >> 2
                    fb = self.stacked_tm.ap()[:, c, 32 + h:33 + h]
                    if hh % 2 == 0:
                        ktoks[hh] = E("vector", lambda e, hh=hh, kvh=kvh,
                                      fb=fb, c=c: e.tensor_scalar(
                            ktl2[hh], self.kT[:, c, kvh * 64:kvh * 64 + 64],
                            fb, None, ALU.mult),
                            waits=[("sV", self.stacked_ready), kv_ready,
                                   ("sPE", ktl_war[hh])])
                        ksem[hh] = "sV"
                    else:
                        ktoks[hh] = E("scalar", lambda e, hh=hh, kvh=kvh,
                                      fb=fb, c=c: e.activation(
                            ktl2[hh], self.kT[:, c, kvh * 64:kvh * 64 + 64],
                            AF.Identity, scale=fb),
                            waits=[("sV", self.stacked_ready), kv_ready,
                                   ("sPE", ktl_war[hh])])
                        ksem[hh] = "sA"

                for bk in range(4):
                    esl = Et[b].rearrange("p (h l) -> p h l", h=32)[
                        :, half * 16 + bk * 4:half * 16 + bk * 4 + 4, :]
                    E("vector", lambda e, bk=bk, esl=esl, b=b: e.tensor_mul(
                        Mt[b][:, bk * 512:(bk + 1) * 512],
                        self.pb[bk].ap().rearrange("p (h l) -> p h l", h=4),
                        esl),
                        waits=[("sPE", mm1_done[half]),
                               ("sA", etoks[half])] +
                        (m_war[b] if bk == 0 else []))
                mmul_done = t.now("sV")

                for hh in range(16):
                    h = half * 16 + hh
                    kvh = h >> 2
                    b2 = self.qhalf(h)
                    h2 = self.qslot(h)
                    dst = self.pst.ap()[b2 * 64:b2 * 64 + 64,
                                        h2 * 64:h2 * 64 + 64]
                    w = [(ksem[hh], ktoks[hh])]
                    if half == 0 and hh == 0:
                        w += pst_free
                    E("tensor", lambda e, hh=hh, kvh=kvh, dst=dst, c=c,
                        b2=b2: e.matmul(
                        dst, ktl2[hh], self.vT[:, c, kvh * 64:kvh * 64 + 64],
                        start=True, stop=True,
                        tile_position=(0, 64) if b2 else (0, 0)),
                        waits=w, inc="sPE")
                    mm3_done[half] = t.now("sPE")
                    ktl_war[hh] = mm3_done[half]

                    ydbank = self.pb[4 + (hh // 8)].ap()
                    w2 = []
                    if hh == 0:
                        w2 = [("sV", mmul_done)] + (yd_free if half == 0
                                                    else [])
                    E("tensor", lambda e, hh=hh, h=h, b=b, ydbank=ydbank, c=c:
                        e.matmul(
                        ydbank[:, (hh % 8) * 64:(hh % 8) * 64 + 64],
                        Mt[b][:, hh * 128:(hh + 1) * 128],
                        self.vT[:, c, (h >> 2) * 64:(h >> 2) * 64 + 64],
                        start=True, stop=True),
                        waits=w2, inc="sPE")
                mm2_done = t.now("sPE")
                m_war[b] = [("sPE", mm2_done)]

                for bk in range(2):
                    eng = "vector" if bk == 0 else "scalar"
                    fn = (lambda e, bk=bk, half=half: e.tensor_copy(
                        ydt[:, half * 1024 + bk * 512:
                            half * 1024 + (bk + 1) * 512],
                        self.pb[4 + bk].ap())) if bk == 0 else \
                        (lambda e, bk=bk, half=half: e.activation(
                            ydt[:, half * 1024 + bk * 512:
                                half * 1024 + (bk + 1) * 512],
                            self.pb[4 + bk].ap(), AF.Identity))
                    E(eng, fn, waits=[("sPE", mm2_done)] +
                      ([("ydw", t.now("ydw"))] if half == 0 else []))
                yd_free = [("sV", t.now("sV")), ("sA", t.now("sA"))]

            ydw_tok = E("sync", lambda e, c=c: e.dma_start(
                out=self.ydD[c], in_=ydt),
                waits=[("sV", t.now("sV")), ("sA", t.now("sA"))],
                inc="ydw", inc_amt=16)

            stok = E("scalar", lambda e, c=c: e.activation(
                self.staging[:, c, :], self.pst.ap(), AF.Identity),
                waits=[("sPE", mm3_done[1])] +
                (self.tern_ring_done if c == 0 else []))
            pst_free = [("sA", stok)]

            if c == 0:
                scan_tok = E("vector", lambda e: e.tensor_copy(
                    R, self.staging[:, 0, :]), waits=[("sA", stok)])
            else:
                E("vector", lambda e, b=b: e.tensor_mul(R, R, dmul[b]),
                  waits=[(f"dm{b}", dm_tok[c]), ("sA", stok)])
                scan_tok = E("vector", lambda e, c=c: e.tensor_add(
                    R, R, self.staging[:, c, :]))
            dm_war[b] = [("sV", scan_tok)]
            q_war[b] = [("sPE", mm1_done[1])]
            e_war[b] = [("sV", t.now("sV"))]

            if c + 2 < NCH:
                prefetch(c + 2)

        ccw1 = E("sync", lambda e: e.dma_start(
            out=self.ccinD[:, 0:1024], in_=R),
            waits=[("sV", scan_tok)], inc="ccw", inc_amt=16)
        E("sync", lambda e: e.dma_start(
            out=self.ccinD[:, 1024:1056], in_=self.drep_sb[:]),
            waits=[self.drep_ready], inc="ccw", inc_amt=16)
        if _SKIP_CC:
            pass
        elif _FAKE_CC:
            for r in range(8):
                E("gpsimd", lambda e, r=r: e.dma_start(
                    out=self.ccoutD[r * 128:(r + 1) * 128, :],
                    in_=self.ccinD),
                    waits=[("ccw", t.now("ccw"))], inc="sCC", inc_amt=16,
                    dma=True)
        else:
            E("gpsimd", lambda e: e.collective_compute(
                "AllGather", ALU.bypass, replica_groups=[list(range(8))],
                ins=[self.ccinD.opt()], outs=[self.ccoutD.opt()]),
                waits=[("ccw", t.now("ccw"))], inc="sCC", inc_amt=1,
                dma=True)
        self.scan1_R = R
        self.s2_dmul = dmul
        self.s2_dm_war = dm_war
        self.s2_qbuf = qbuf

    def s4a(self):
        E, t = self.tape.emit, self.tape

        wslab = [self.carve(self.A4, 0, [128, 2048], F32),
                 self.carve(self.A4, 8192, [128, 2048], F32)]
        tslab = [self.carve(self.A4, 24576, [128, 2048], BF16),
                 self.carve(self.A4, 28672, [128, 2048], BF16)]
        abss = self.carve(self.A4, 24576, [128, 80], F32)
        sclr = self.carve(self.A4, 25600, [1, 288], F32)
        tscr = self.carve(self.A4, 16384, [128, 2048], F32)

        ws_war = {i: [] for i in range(2)}
        if _FAKE_CC:
            inter = [("u", i >> 1, i & 1) for i in range(32)] + \
                    [("d", i, 0) for i in range(32)]
        else:
            inter = [("u", si >> 1, si & 1) for si in range(4)] + \
                    [("d", si, 0) for si in range(4)]

        tpoU = self.ternPoutU.rearrange("(r c) -> r c", c=TUPH)
        tpoD = self.ternPoutD.rearrange("(r c) -> r c", c=TDNH)
        tpi_u = self.ternPinU.rearrange(
            "(s p f) -> s p f", s=2, p=128, f=4096)
        tpi_d = self.ternPinD.rearrange(
            "(s p f) -> s p f", s=4, p=128, f=2048)

        def slab_src(j):
            m, i, hv = j
            if _FAKE_CC:
                if m == "u":
                    return self.wupT[i * 128:(i + 1) * 128,
                                     hv * 2048:(hv + 1) * 2048]
                return self.wdnT[i * 128:(i + 1) * 128, :]
            if m == "u":
                return self.wupP[i * 128:(i + 1) * 128,
                                 hv * 2048:(hv + 1) * 2048]
            return self.wdnP[i * 128:(i + 1) * 128, :]

        def slab_dst(j):
            m, i, hv = j
            if _FAKE_CC:
                if m == "u":
                    r, s = i >> 1, i & 1
                    return tpoU[r].rearrange(
                        "(s p f) -> s p f", s=2, p=128, f=4096)[
                        s, :, hv * 2048:(hv + 1) * 2048]
                r, s = i >> 2, i & 3
                return tpoD[r].rearrange(
                    "(s p f) -> s p f", s=4, p=128, f=2048)[s]
            if m == "u":
                return tpi_u[i][:, hv * 2048:(hv + 1) * 2048]
            return tpi_d[i]

        for n, j in enumerate(inter):
            b = n % 2
            tok = E("sync", lambda e, j=j, b=b: e.dma_start(
                out=wslab[b], in_=slab_src(j)),
                waits=ws_war[b], inc=f"aw{b}", inc_amt=16)
            atok = E("scalar", lambda e, b=b, n=n: e.activation(
                tscr, wslab[b], AF.Abs,
                accum_out=abss[:, n:n + 1]),
                waits=[(f"aw{b}", tok)])
            ws_war[b] = [("sA", atok)]
        a_all = t.now("sA")
        nu = len(inter) // 2
        vu = E("vector", lambda e, nu=nu: e.tensor_reduce(
            abss[:, 64:65], abss[:, 0:nu],
            mybir.AxisListType.X, ALU.add), waits=[("sA", a_all)])
        vd = E("vector", lambda e, nu=nu: e.tensor_reduce(
            abss[:, 65:66], abss[:, nu:2 * nu],
            mybir.AxisListType.X, ALU.add))

        t1 = E("gpsimd", lambda e: e.dma_start(
            out=self.sclbD.rearrange("(p c) -> p c", c=2),
            in_=abss[:, 64:66]),
            waits=[("sV", t.now("sV"))], inc="auw", inc_amt=16, dma=True)
        t2 = E("gpsimd", lambda e: e.dma_start(
            out=sclr[0:1, 0:256].rearrange("o (c p) -> o c p", c=2),
            in_=self.sclbD.rearrange("(p c) -> c p", c=2).unsqueeze(0),
            allow_slow_non_contiguous=True),
            waits=[("auw", t1)], inc="aur", inc_amt=16, dma=True)
        rtok = E("vector", lambda e: e.tensor_reduce(
            sclr[0:1, 256:258],
            sclr[0:1, 0:256].rearrange("o (c p) -> o c p", c=2),
            mybir.AxisListType.X, ALU.add), waits=[("aur", t2)])
        t3 = E("gpsimd", lambda e: e.dma_start(
            out=self.ccSclIn.rearrange("(o a) -> o a", o=1),
            in_=sclr[0:1, 256:258]),
            waits=[("sV", rtok)], inc="scw", inc_amt=16, dma=True)
        if _FAKE_CC:
            for r in range(8):
                E("gpsimd", lambda e, r=r: e.dma_start(
                    out=self.ccSclOut[r * 2:(r + 1) * 2]
                    .rearrange("(o a) -> o a", o=1),
                    in_=self.ccSclIn.rearrange("(o a) -> o a", o=1)),
                    waits=[("scw", t3)] if r == 0 else [], inc="sCS",
                    inc_amt=16, dma=True)
            scs_wait = [("sCS", t.now("sCS"))]
        else:
            E("gpsimd", lambda e: e.collective_compute(
                "AllGather", ALU.bypass, replica_groups=[list(range(8))],
                ins=[self.ccSclIn.opt()], outs=[self.ccSclOut.opt()]),
                waits=[("scw", t3)], inc="sCS", inc_amt=1, dma=True)
            scs_wait = [("sCS", 1)]
        t4 = E("gpsimd", lambda e: e.dma_start(
            out=sclr[0:1, 258:274].rearrange("o (c r) -> o c r", c=2),
            in_=self.ccSclOut.rearrange("(r c) -> c r", c=2).unsqueeze(0),
            allow_slow_non_contiguous=True),
            waits=scs_wait, inc="scr", inc_amt=16, dma=True)
        stok = E("vector", lambda e: e.tensor_reduce(
            sclr[0:1, 274:276],
            sclr[0:1, 258:274].rearrange("o (c r) -> o c r", c=2),
            mybir.AxisListType.X, ALU.add), waits=[("scr", t4)])
        if _FAKE_CC:
            stok = E("vector", lambda e: e.tensor_scalar(
                sclr[0:1, 274:276], sclr[0:1, 274:276], 1.0 / 8.0, None,
                ALU.mult))
        t5 = E("gpsimd", lambda e: e.dma_start(
            out=self.swD.rearrange("(o a) -> o a", o=1),
            in_=sclr[0:1, 274:276]),
            waits=[("sV", stok)], inc="sws", inc_amt=16, dma=True)
        t6 = E("gpsimd", lambda e: e.dma_start(
            out=self.sw_sb[:, 0:2],
            in_=self.swD.rearrange("(o a) -> o a", o=1)
            .broadcast_to([128, 2])),
            waits=[("sws", t5)], inc="swb", inc_amt=16, dma=True)
        E("vector", lambda e: e.tensor_scalar(
            self.sw_sb[:, 0:2], self.sw_sb[:, 0:2], 1.0 / (D * F2), EPS,
            ALU.mult, ALU.max), waits=[("swb", t6)])
        su_tok = t.now("sV")
        inv_up = self.inv_sb.ap()[:, 0:1]
        inv_dn = self.inv_sb.ap()[:, 1:2]
        E("vector", lambda e: e.reciprocal(
            inv_up, self.sw_sb.ap()[:, 0:1]), waits=[("sV", su_tok)])
        E("vector", lambda e: e.reciprocal(
            inv_dn, self.sw_sb.ap()[:, 1:2]))
        inv_tok = t.now("sV")

        tscr_war = []
        tsl_war = {i: [] for i in range(2)}
        for n, j in enumerate(inter):
            b = n % 2
            tok = E("sync", lambda e, j=j, b=b: e.dma_start(
                out=wslab[b], in_=slab_src(j)),
                waits=ws_war[b], inc=f"tw{b}", inc_amt=16)
            inv = inv_up if j[0] == "u" else inv_dn
            a1 = E("scalar", lambda e, b=b, inv=inv: e.activation(
                tscr, wslab[b], AF.Identity, scale=inv,
                bias=self.csts[:, 2:3]),
                waits=[(f"tw{b}", tok), ("sV", inv_tok)] + tscr_war)
            ws_war[b] = [("sA", a1)]
            mintok = E("vector", lambda e, b=b: e.tensor_scalar(
                tslab[b], tscr, MAGIC, 1.0, ALU.subtract, ALU.min),
                waits=[("sA", a1)] + tsl_war[b])
            tscr_war = [("sV", mintok)]
            vtok = E("vector", lambda e, b=b: e.tensor_scalar(
                tslab[b], tslab[b], -1.0, None, ALU.max))
            wsem = ("tuw" if j[0] == "u" else "tdw") + str(b)
            wtok = E("gpsimd", lambda e, j=j, b=b: e.dma_start(
                out=slab_dst(j), in_=tslab[b]),
                waits=[("sV", vtok)], inc=wsem, inc_amt=16, dma=True)
            tsl_war[b] = [(wsem, wtok)]
        tuw_toks = [(f"tuw{b}", t.now(f"tuw{b}")) for b in range(2)]
        tdw_toks = [(f"tdw{b}", t.now(f"tdw{b}")) for b in range(2)]
        if _FAKE_CC:
            up_cache_wait = list(tuw_toks)
            dn_cache_wait = list(tdw_toks)
        else:
            E("gpsimd", lambda e: e.collective_compute(
                "AllGather", ALU.bypass, replica_groups=[list(range(8))],
                ins=[self.ternPinU.opt()], outs=[self.ternPoutU.opt()]),
                waits=tuw_toks, inc="sCU", inc_amt=1, dma=True)
            E("gpsimd", lambda e: e.collective_compute(
                "AllGather", ALU.bypass, replica_groups=[list(range(8))],
                ins=[self.ternPinD.opt()], outs=[self.ternPoutD.opt()]),
                waits=tdw_toks, inc="sCD", inc_amt=1, dma=True)
            up_cache_wait = [("sCU", 1)]
            dn_cache_wait = [("sCD", 1)]
        tern_dve_done = t.now("sV")
        tern_act_done = t.now("sA")
        E("vector", lambda e: e.tensor_scalar(
            self.sw_sb[:, 0:1], self.sw_sb[:, 0:1],
            1.0 / (127.0 * float(np.sqrt(2048.0))), None, ALU.mult))
        E("vector", lambda e: e.tensor_scalar(
            self.sw_sb[:, 1:2], self.sw_sb[:, 1:2], 1.0 / 127.0, None,
            ALU.mult))
        self.sw_ready = t.now("sV")
        self.tern_tok = [("sV", tern_dve_done), ("sA", tern_act_done)]
        self.tuw_toks = tuw_toks
        self.tdw_toks = tdw_toks
        self.up_cache_wait = up_cache_wait
        self.dn_cache_wait = dn_cache_wait
        self.tpoU_v = tpoU
        self.tpoD_v = tpoD
        self.tern_ring_done = tuw_toks + tdw_toks + self.tern_tok


    def s4(self):
        E, t = self.tape.emit, self.tape
        t.stage_barrier()

        xnq0 = self.carve(self.A2, 0, [128, NDC, 512], BF16)
        xr0 = E("sync", lambda e: e.dma_start(
            out=xnq0, in_=self.xnD[:, :, 0:512]
            .transpose([1, 0, 2])),
            waits=[self.xnw_tok], inc="xnr0", inc_amt=16)

        tern_tok = self.tern_tok
        tuw_toks = self.tuw_toks
        tdw_toks = self.tdw_toks
        up_cache_wait = self.up_cache_wait
        dn_cache_wait = self.dn_cache_wait
        sw_ready = self.sw_ready
        tpoU = self.tpoU_v
        tpoD = self.tpoD_v

        xnq = self.carve(self.A2, 0, [128, NDC, 512], BF16)
        u1 = self.carve(self.A2, 16384, [128, 2048], BF16)
        pong1 = self.carve(self.A2, 20480, [128, 2048], BF16)
        y1 = self.carve(self.A2, 24576, [128, 2048], F32)
        xqf = self.carve(self.A2, 32768, [128, NDC, 512], BF16)
        terns = [self.carve(self.A2, 49152, [128, 16, 128], BF16),
                 self.carve(self.A2, 53248, [128, 16, 128], BF16)]
        xq1t = self.carve(self.A2, 57344, [128, 2048], BF16)
        ternsD = [self.carve(self.A2, 49152, [128, 32, 128], BF16),
                  self.carve(self.A2, 57344, [128, 32, 128], BF16)]
        xq2f = self.carve(self.A2, 0, [128, NF2C, 512], BF16)
        h2q = self.carve(self.A1, 0, [128, NF2C, 512], BF16)
        u2 = self.carve(self.A1, 32768, [128, 4096], BF16)
        pong2 = self.carve(self.A1, 40960, [128, 4096], BF16)
        y2h = self.carve(self.A1, 49152, [128, 2048], F32)
        xq2t = self.carve(self.A1, 57344, [128, 4096], BF16)
        rawq = [self.carve(self.A1, 0, [128, 512], BF16),
                self.carve(self.A1, 1024, [128, 512], BF16)]
        mcol = self.carve(self.A3, 16384, [128, 8], F32)
        betac = self.carve(self.A3, 16416, [128, 1], F32)

        banks6 = [self.pb[i].ap() for i in range(6)]
        banks8 = banks6 + [self.pst.ap()[:, 0:512],
                           self.pst.ap()[:, 512:1024]]
        bank_free = {i: [] for i in range(8)}

        xnr_war = []
        u1_war = list(tern_tok)
        pong1_war = list(tern_tok)
        u2_war = tuw_toks + tdw_toks + tern_tok
        pong2_war = tuw_toks + tdw_toks + list(tern_tok)
        tern_war = {0: [], 1: []}
        xqf_war = []
        xq2f_war = []
        raw_war = {0: list(tern_tok), 1: list(tern_tok)}
        y1_war = []
        y2_war = []
        xq1t_war = []
        xq2t_war = list(tern_tok) + tuw_toks + tdw_toks
        h2_war = list(tern_tok) + tuw_toks + tdw_toks

        wT_up4 = tpoU.rearrange(
            "r (s p f) -> r s p f", s=2, p=128, f=4096)
        wT_dn4 = tpoD.rearrange(
            "r (s p f) -> r s p f", s=4, p=128, f=2048)

        for q in range(NQ):
            xb = q % 2
            if q == 0:
                xr = xr0
            else:
                xr = E("sync", lambda e, q=q: e.dma_start(
                    out=xnq, in_=self.xnD[:, :, q * 512:(q + 1) * 512]
                    .transpose([1, 0, 2])),
                    waits=[self.xnw_tok] + xnr_war,
                    inc=f"xnr{xb}", inc_amt=16)
            xr_tok = (f"xnr{xb}", xr)

            for ti in range(4):
                tt = q * 4 + ti
                bu = ti % 2
                g0 = 0 if bu == 0 else 4
                for c in range(NDC):
                    w = [xr_tok]
                    if c == 0:
                        w = [xr_tok] + bank_free[g0] + bank_free[g0 + 1] + \
                            bank_free[g0 + 2] + bank_free[g0 + 3]
                    E("tensor", lambda e, c=c, ti=ti, g0=g0: e.matmul(
                        banks8[g0 + c % 4][:, (c // 4) * 128:
                                           (c // 4 + 1) * 128],
                        xnq[:, c, ti * 128:(ti + 1) * 128], self.h128_sb[:],
                        start=True, stop=True),
                        waits=w, inc="sPE")
                f1_done = t.now("sPE")
                for bk in range(4):
                    eng = "vector" if bk % 2 == 0 else "scalar"
                    sem = "sV" if bk % 2 == 0 else "sA"
                    fn = (lambda e, bk=bk, g0=g0: e.tensor_copy(
                        u1.rearrange("p (c l) -> p c l", c=16)
                        [:, bk::4, :], banks8[g0 + bk]
                        .rearrange("p (s l) -> p s l", s=4))) \
                        if bk % 2 == 0 else \
                        (lambda e, bk=bk, g0=g0: e.activation(
                            u1.rearrange("p (c l) -> p c l", c=16)
                            [:, bk::4, :], banks8[g0 + bk]
                            .rearrange("p (s l) -> p s l", s=4),
                            AF.Identity))
                    E(eng, fn, waits=[("sPE", f1_done)] +
                      (u1_war if bk == 0 else []))
                    bank_free[g0 + bk] = [(sem, t.now(sem))]
                evac_sA = t.now("sA")
                bufs1 = [u1, pong1]
                for hi, hstep in enumerate((8, 4, 2, 1)):
                    iv = bufs1[hi % 2].rearrange(
                        "p (g two h l) -> p g two h l", two=2, h=hstep,
                        l=128)
                    ov = bufs1[(hi + 1) % 2].rearrange(
                        "p (g two h l) -> p g two h l", two=2, h=hstep,
                        l=128)
                    a = iv[:, :, 0, :, :]
                    bt = iv[:, :, 1, :, :]
                    w = [("sA", evac_sA)] + (pong1_war if hi == 0 else [])
                    E("vector", lambda e, a=a, bt=bt, ov=ov: e.tensor_add(
                        ov[:, :, 0, :, :], a, bt),
                        waits=w if hi == 0 else [])
                    E("vector", lambda e, a=a, bt=bt, ov=ov: e.tensor_tensor(
                        ov[:, :, 1, :, :], a, bt, ALU.subtract))
                pong1_war = []
                E("vector", lambda e: e.reduce_max(
                    mcol[:, 0:1], u1, mybir.AxisListType.X,
                    apply_absolute_value=True))
                E("vector", lambda e: e.tensor_scalar(
                    mcol[:, 0:1], mcol[:, 0:1], EPS, None, ALU.max))
                E("vector", lambda e: e.reciprocal(mcol[:, 1:2],
                                                   mcol[:, 0:1]))
                E("vector", lambda e: e.tensor_scalar(
                    mcol[:, 1:2], mcol[:, 1:2], 127.0, None, ALU.mult))
                mready = t.now("sV")
                E("scalar", lambda e: e.activation(
                    y1, u1, AF.Identity, scale=mcol[:, 1:2],
                    bias=self.csts[:, 2:3]),
                    waits=y1_war + [("sV", mready)])
                u1_war = [("sA", t.now("sA"))]
                ytok = E("scalar", lambda e: e.activation(
                    xq1t, y1, AF.Identity, bias=self.csts[:, 3:4]),
                    waits=xq1t_war)
                y1_war = [("sA", ytok)]
                E("vector", lambda e: e.tensor_mul(
                    mcol[:, 2:3], mcol[:, 0:1], self.sw_sb.ap()[:, 0:1]),
                    waits=[("sV", sw_ready)])
                E("vector", lambda e, tt=tt: e.tensor_mul(
                    self.alpha_sb[:, tt:tt + 1], mcol[:, 2:3], mcol[:, 2:3]))
                xtok = E("sync", lambda e, ti=ti: e.dma_start_transpose(
                    xqf[:, :, ti * 128:(ti + 1) * 128], xq1t),
                    waits=[("sA", ytok)] + xqf_war, inc="sX", inc_amt=16)
                xq1t_war = [("sX", xtok)]
            xq_ready = t.now("sX")
            xnr_war = [("sPE", t.now("sPE"))]

            for f2c in range(NF2C):
                b = f2c % 2
                for s_ in range(2):
                    ttok = E("sync", lambda e, f2c=f2c, b=b, s_=s_:
                             e.dma_start(
                        out=terns[b].rearrange("p (r s) f -> p r s f", r=8)
                        [:, :, s_, :],
                        in_=wT_up4[:, s_, :, f2c * 128:(f2c + 1) * 128]
                        .transpose([1, 0, 2])),
                        waits=(up_cache_wait + tern_war[b])
                        if s_ == 0 else (), inc=f"tu{b}", inc_amt=16)
                bkid = f2c % 6
                bank = banks6[bkid]
                for dc in range(NDC):
                    w = []
                    if dc == 0:
                        w = [(f"tu{b}", ttok), ("sX", xq_ready)] + \
                            bank_free[bkid]
                    E("tensor", lambda e, dc=dc, b=b, bank=bank: e.matmul(
                        bank, terns[b][:, dc, :], xqf[:, dc, :],
                        start=(dc == 0), stop=(dc == NDC - 1)),
                        waits=w, inc="sPE")
                up_done = t.now("sPE")
                tern_war[b] = [("sPE", up_done)]
                atok = E("scalar", lambda e, f2c=f2c, bank=bank: e.activation(
                    h2q[:, f2c, :], bank, AF.Relu),
                    waits=[("sPE", up_done)] + (h2_war if f2c == 0 else []))
                ptok = E("gpsimd", lambda e, f2c=f2c: e.tensor_mul(
                    h2q[:, f2c, :], h2q[:, f2c, :], h2q[:, f2c, :]),
                    waits=[("sA", atok)])
                bank_free[bkid] = [("sP", ptok)]
            h2_done = t.now("sP")

            for ti in range(4):
                tt = q * 4 + ti
                bu = ti % 2
                allfree = []
                for i in range(8):
                    allfree += bank_free[i]
                for c2 in range(NF2C):
                    w = [("sP", h2_done)]
                    if c2 == 0:
                        w += allfree
                    E("tensor", lambda e, c2=c2, ti=ti: e.matmul(
                        banks8[c2 % 8][:, (c2 // 8) * 128:
                                       (c2 // 8 + 1) * 128],
                        h2q[:, c2, ti * 128:(ti + 1) * 128], self.h128_sb[:],
                        start=True, stop=True),
                        waits=w, inc="sPE")
                f2_done = t.now("sPE")
                for bk in range(8):
                    eng = "vector" if bk % 2 == 0 else "scalar"
                    sem = "sV" if bk % 2 == 0 else "sA"
                    fn = (lambda e, bk=bk: e.tensor_copy(
                        u2.rearrange("p (c l) -> p c l", c=32)
                        [:, bk::8, :], banks8[bk]
                        .rearrange("p (s l) -> p s l", s=4))) \
                        if bk % 2 == 0 else \
                        (lambda e, bk=bk: e.activation(
                            u2.rearrange("p (c l) -> p c l", c=32)
                            [:, bk::8, :], banks8[bk]
                            .rearrange("p (s l) -> p s l", s=4),
                            AF.Identity))
                    E(eng, fn, waits=[("sPE", f2_done)] +
                      (u2_war if bk == 0 else []))
                    bank_free[bk] = [(sem, t.now(sem))]
                evac2_sA = t.now("sA")
                bufs2 = [u2, pong2]
                for hi, hstep in enumerate((16, 8, 4, 2, 1)):
                    iv = bufs2[hi % 2].rearrange(
                        "p (g two h l) -> p g two h l", two=2, h=hstep,
                        l=128)
                    ov = bufs2[(hi + 1) % 2].rearrange(
                        "p (g two h l) -> p g two h l", two=2, h=hstep,
                        l=128)
                    a = iv[:, :, 0, :, :]
                    bt = iv[:, :, 1, :, :]
                    w = [("sA", evac2_sA)] + \
                        (pong2_war + y2_war if hi == 0 else [])
                    E("vector", lambda e, a=a, bt=bt, ov=ov: e.tensor_add(
                        ov[:, :, 0, :, :], a, bt),
                        waits=w if hi == 0 else [])
                    E("vector", lambda e, a=a, bt=bt, ov=ov: e.tensor_tensor(
                        ov[:, :, 1, :, :], a, bt, ALU.subtract))
                pong2_war = []
                u2_war = [("sV", t.now("sV"))]
                E("vector", lambda e: e.reduce_max(
                    mcol[:, 3:4], pong2, mybir.AxisListType.X,
                    apply_absolute_value=True))
                E("vector", lambda e, tt=tt: e.tensor_mul(
                    mcol[:, 4:5], mcol[:, 3:4],
                    self.alpha_sb.ap()[:, tt:tt + 1]))
                E("vector", lambda e: e.tensor_scalar(
                    mcol[:, 4:5], mcol[:, 4:5], 1.0 / 64.0, EPS,
                    ALU.mult, ALU.max))
                E("vector", lambda e: e.reciprocal(mcol[:, 5:6],
                                                   mcol[:, 4:5]))
                E("vector", lambda e, tt=tt: e.tensor_mul(
                    mcol[:, 6:7], mcol[:, 5:6],
                    self.alpha_sb.ap()[:, tt:tt + 1]))
                E("vector", lambda e: e.tensor_scalar(
                    mcol[:, 6:7], mcol[:, 6:7], 127.0 / 64.0, None,
                    ALU.mult))
                btok = E("vector", lambda e: e.tensor_mul(
                    betac, mcol[:, 4:5], self.sw_sb.ap()[:, 1:2]),
                    waits=[("btw", t.now("btw"))])
                E("sync", lambda e, tt=tt: e.dma_start(
                    out=self.betaD[tt * 128:(tt + 1) * 128]
                    .rearrange("(p o) -> p o", o=1), in_=betac),
                    waits=[("sV", btok)], inc="btw", inc_amt=16)
                m2ready = t.now("sV")
                for hf in range(2):
                    E("scalar", lambda e, hf=hf: e.activation(
                        y2h, pong2[:, hf * 2048:(hf + 1) * 2048],
                        AF.Identity, scale=mcol[:, 6:7],
                        bias=self.csts[:, 2:3]),
                        waits=[("sV", m2ready)])
                    ytok2 = E("scalar", lambda e, hf=hf: e.activation(
                        xq2t[:, hf * 2048:(hf + 1) * 2048], y2h, AF.Identity,
                        bias=self.csts[:, 3:4]),
                        waits=xq2t_war if hf == 0 else [])
                    y2_war = [("sA", ytok2)]
                x2tok = E("sync", lambda e, ti=ti: e.dma_start_transpose(
                    xq2f[:, :, ti * 128:(ti + 1) * 128], xq2t),
                    waits=[("sA", t.now("sA"))] + xq2f_war, inc="sX",
                    inc_amt=16)
                xq2t_war = [("sX", x2tok)]
            xq2_ready = t.now("sX")
            h2_war = [("sPE", t.now("sPE"))]

            for dc in range(NDC):
                b = dc % 2
                for s_ in range(4):
                    ttok = E("sync", lambda e, dc=dc, b=b, s_=s_:
                             e.dma_start(
                        out=ternsD[b].rearrange("p (r s) f -> p r s f", r=8)
                        [:, :, s_, :],
                        in_=wT_dn4[:, s_, :, dc * 128:(dc + 1) * 128]
                        .transpose([1, 0, 2])),
                        waits=(dn_cache_wait + tern_war[b])
                        if s_ == 0 else (), inc=f"td{b}", inc_amt=16)
                bkid = dc % 6
                bank = banks6[bkid]
                for c2 in range(NF2C):
                    w = []
                    if c2 == 0:
                        w = [(f"td{b}", ttok), ("sX", xq2_ready)] + \
                            bank_free[bkid]
                    E("tensor", lambda e, c2=c2, b=b, bank=bank: e.matmul(
                        bank, ternsD[b][:, c2, :], xq2f[:, c2, :],
                        start=(c2 == 0), stop=(c2 == NF2C - 1)),
                        waits=w, inc="sPE")
                dn_done = t.now("sPE")
                tern_war[b] = [("sPE", dn_done)]
                rb = dc % 2
                rtok = E("vector", lambda e, rb=rb, bank=bank: e.tensor_copy(
                    rawq[rb], bank), waits=[("sPE", dn_done)] + raw_war[rb])
                bank_free[bkid] = [("sV", rtok)]
                wtok = E("sync", lambda e, dc=dc, q=q, rb=rb: e.dma_start(
                    out=self.rawD[dc, :, q * 512:(q + 1) * 512],
                    in_=rawq[rb]),
                    waits=[("sV", rtok)], inc=f"rdw{rb}", inc_amt=16)
                raw_war[rb] = [(f"rdw{rb}", wtok)]
            xq2f_war = [("sPE", t.now("sPE"))]
            xqf_war = [("sPE", t.now("sPE"))]
            h2_war = [("sPE", t.now("sPE")),
                      ("rdw0", t.now("rdw0")), ("rdw1", t.now("rdw1"))]

        self.rdw_toks = [("rdw0", t.now("rdw0")), ("rdw1", t.now("rdw1"))]
        self.btw_tok = ("btw", t.now("btw"))

    def s5(self):
        E, t = self.tape.emit, self.tape
        t.stage_barrier()
        R = self.scan1_R
        dmul = self.s2_dmul
        self.St0 = self.carve(self.A1, 29184, [128, 1024], BF16)
        Tk = [self.carve(self.A2, 36864, [128, 1024], F32),
              self.carve(self.A2, 40960, [128, 1024], F32)]
        Mk = self.carve(self.A2, 45056, [128, 1024], F32)
        mcols = self.carve(self.A2, 49152, [128, 16], F32)
        rhs_c = [self.carve(self.A2, 49280, [1, 32], F32),
                 self.carve(self.A2, 49408, [1, 32], F32)]
        ones64 = self.carve(self.A2, 49536, [1, 64], F32)

        cc = [("sCC", 1)]
        dtok = E("sync", lambda e: e.dma_start(
            out=self.dall_sb[0:1, 0:256],
            in_=self.ccoutD.rearrange("(r p) c -> r p c", p=128)
            [:, 0:1, 1024:1056]),
            waits=cc, inc="dal", inc_amt=16)
        E("vector", lambda e: e.memset(R, 0.0),
          waits=[("sV", t.now("sV"))])
        ones_tok = E("vector", lambda e: e.memset(ones64, 1.0))
        tk_war = {0: [], 1: []}
        rhs_war = {0: [], 1: []}
        lt_tok = {}
        pe_tok = {}
        pst_war = []
        mul_fn = {}

        def tk_load(k):
            kb = k % 2
            lt_tok[k] = E("sync", lambda e, k=k, kb=kb: e.dma_start(
                out=Tk[kb], in_=self.ccoutD[k * 128:(k + 1) * 128, 0:1024]),
                waits=cc + tk_war[kb], inc=f"tk{kb}", inc_amt=16)
        tk_load(0)
        tk_load(1)
        for k in range(8):
            kb = k % 2
            E("vector", lambda e, k=k: e.tensor_scalar(
                self.mr_sb[0:1, 0:32], self.dall_sb[0:1, 0:32],
                self.em_sb[0:1, k * 8 + 0:k * 8 + 1],
                self.om_sb[0:1, k * 8 + 0:k * 8 + 1],
                ALU.mult, ALU.add), waits=[("dal", dtok)] if k == 0 else [])
            for m in range(1, 8):
                E("vector", lambda e, k=k, m=m: e.tensor_scalar(
                    self.mr_sb[0:1, 32:64],
                    self.dall_sb[0:1, m * 32:(m + 1) * 32],
                    self.em_sb[0:1, k * 8 + m:k * 8 + m + 1],
                    self.om_sb[0:1, k * 8 + m:k * 8 + m + 1],
                    ALU.mult, ALU.add))
                E("vector", lambda e: e.tensor_mul(
                    self.mr_sb[0:1, 0:32], self.mr_sb[0:1, 0:32],
                    self.mr_sb[0:1, 32:64]))
            E("vector", lambda e, k=k: e.tensor_scalar(
                self.mr_sb[0:1, 0:32], self.mr_sb[0:1, 0:32],
                self.sel_sb[0:1, k:k + 1], None, ALU.mult))
            mv = self.mr_sb[0:1, 0:32].rearrange(
                "o (kh2 b2 j) -> o b2 kh2 j", kh2=4, b2=2)
            ctok = E("vector", lambda e, mv=mv, kb=kb: e.tensor_copy(
                rhs_c[kb].rearrange("o (b2 kh2 j) -> o b2 kh2 j",
                                    b2=2, kh2=4),
                mv), waits=rhs_war[kb])

            def mk_mul(k=k, kb=kb, ctok=ctok):
                nonlocal pst_war
                for b2 in range(2):
                    E("tensor", lambda e, b2=b2, kb=kb: e.matmul(
                        self.pst.ap()[b2 * 64:(b2 + 1) * 64, 0:16],
                        ones64, rhs_c[kb][0:1, b2 * 16:(b2 + 1) * 16],
                        start=True, stop=True,
                        tile_position=(0, 64) if b2 else (0, 0)),
                        waits=([("sV", ctok), ("sV", ones_tok)] + pst_war)
                        if b2 == 0 else [], inc="sPE")
                pe_tok[k] = t.now("sPE")
                rhs_war[kb] = [("sPE", pe_tok[k])]
                etok = E("vector", lambda e: e.tensor_copy(
                    mcols, self.pst.ap()[:, 0:16]),
                    waits=[("sPE", pe_tok[k])])
                pst_war = [("sV", etok)]
                mcv = mcols.unsqueeze(2).broadcast_to([128, 16, P])
                E("vector", lambda e, kb=kb, mcv=mcv: e.tensor_tensor(
                    Mk.rearrange("p (h pp) -> p h pp", h=16),
                    Tk[kb].rearrange("p (h pp) -> p h pp", h=16),
                    mcv, ALU.mult),
                    waits=[(f"tk{kb}", lt_tok[k])])
                multok = t.now("sV")
                E("vector", lambda e: e.tensor_add(R, R, Mk))
                tk_war[kb] = [("sV", multok)]
                if k + 2 < 8:
                    tk_load(k + 2)
            mul_fn[k] = mk_mul
            if k >= 1:
                mul_fn[k - 1]()
        mul_fn[7]()
        rin_done = t.now("sV")

        self._scan_state = dict(R=R, dmul=dmul, dm_war=self.s2_dm_war,
                                rin_done=rin_done)
        self.scan2_done = rin_done

    def St_slot(self, c):
        return self.St0 if c == 0 else self.staging[:, c - 1, :]

    def s6(self):
        E, t = self.tape.emit, self.tape
        t.stage_barrier()
        qbuf = self.s2_qbuf
        E("vector", lambda e: e.memset(qbuf[0], 0.0))
        qz_init6 = E("vector", lambda e: e.memset(qbuf[1], 0.0))
        self.z_fm = self.carve(self.A2, 0, [128, NDC, 2048], BF16)
        ysq = [self.carve(self.A1, 0, [128, 2048], F32),
               self.carve(self.A1, 8192, [128, 2048], F32)]
        y_full = [self.carve(self.A1, 16384, [128, 2048], F32),
                  self.carve(self.A1, 32768, [128, 2048], F32)]
        dmul2 = [self.carve(self.A1, 40960, [128, 1024], F32),
                 self.carve(self.A1, 45056, [128, 1024], F32)]
        z1 = self.carve(self.A1, 49152, [128, 2048], BF16)
        z1T = self.carve(self.A1, 53248, [128, 16, 128], BF16)
        sgg = self.carve(self.A1, 57344, [128, 16, 128], BF16)
        rms = self.carve(self.A1, 61440, [128, 32], F32)
        ydl = [self.carve(self.A3, 16384, [128, 2048], BF16),
               self.carve(self.A3, 20480, [128, 2048], BF16)]
        sg_war = []

        R = self._scan_state["R"]
        dm_war0 = self._scan_state["dm_war"]
        dm_war = {0: dm_war0[0] + dm_war0[1], 1: []}
        dm_tok = {}

        def dm_load(c):
            b = c % 2
            dm_tok[c] = E("sync", lambda e, b=b, c=c: e.dma_start(
                out=dmul2[b], in_=self.ddT[c]
                .rearrange("(b2 rest) -> b2 rest", b2=2)
                .unsqueeze(1).broadcast_to([2, 64, 1024])),
                waits=[self.ddT_ready] + dm_war[b], inc=f"dm6{b}",
                inc_amt=16)

        ydw_all = ("ydw", t.now("ydw"))
        yd_war = {0: [], 1: []}
        yd_tok = {}

        def yd_load(c):
            b = c % 2
            yd_tok[c] = E("sync", lambda e, b=b, c=c: e.dma_start(
                out=ydl[b], in_=self.ydD[c]),
                waits=[ydw_all] + yd_war[b], inc=f"yl{b}", inc_amt=16)

        qv = self.qD \
            .rearrange("c (two p) t -> (c two) p t", two=2) \
            .rearrange("(kh2 half j) p t -> half kh2 j p t",
                       kh2=4, half=2, j=4)
        q_tok = {}

        def q_load(c, extra):
            b = c % 2
            for hf in range(2):
                for kh2 in range(4):
                    h0 = kh2 * 8 + hf * 4
                    tok = E("sync", lambda e, b=b, c=c, hf=hf, kh2=kh2,
                            h0=h0: e.dma_start(
                        out=qbuf[b][hf * 64:(hf + 1) * 64, h0:h0 + 4, :],
                        in_=qv[hf, kh2, :, :, c * 128:(c + 1) * 128]
                        .transpose([1, 0, 2])),
                        waits=(self.qdw_toks + extra +
                               [("sV", qz_init6)])
                        if (hf == 0 and kh2 == 0) else (),
                        inc=f"q2{b}", inc_amt=16)
            q_tok[c] = tok

        dm_load(0)
        dm_load(1)
        yd_load(0)
        yd_load(1)
        q_load(0, [])
        q_load(1, [])

        ytt_hist = {-1: 0}
        z1tok_hist = {-2: 0, -1: 0}
        red_hist = {-2: 0, -1: 0}
        ptok_hist = {-2: 0, -1: 0}
        rtok_hist = {-1: 0}
        st_hist = {}
        ztok_prev = 0
        mtok_prev = 0
        pend_mtok = None

        def scan_step(c):
            b = c % 2
            dst = self.St0 if c == 0 else self.staging[:, c - 1, :]
            st_hist[c] = E("vector", lambda e, dst=dst: e.tensor_copy(
                dst, R))
            E("vector", lambda e, b=b: e.tensor_mul(R, R, dmul2[b]),
              waits=[(f"dm6{b}", dm_tok[c])])
            stok = E("vector", lambda e, c=c: e.tensor_add(
                R, R, self.staging[:, c, :]))
            dm_war[b] = [("sV", stok)]
            if c + 2 < NCH:
                dm_load(c + 2)

        scan_step(0)
        for c in range(NCH):
            b = c % 2
            for h in range(H):
                h2 = self.qslot(h)
                lhs = qbuf[b][:, h, :]
                stc = self.St_slot(c)
                rhs = stc[:, h2 * 64:h2 * 64 + 64]
                w = []
                if h == 0:
                    w = [(f"q2{b}", q_tok[c]), ("sV", st_hist[c]),
                         ("sV", ytt_hist[c - 1])]
                E("tensor", lambda e, h=h, lhs=lhs, rhs=rhs: e.matmul(
                    self.pb[h // 8].ap()[:, (h % 8) * 64:(h % 8) * 64 + 64],
                    lhs, rhs, start=True, stop=True),
                    waits=w, inc="sPE")
            mm4_done = t.now("sPE")
            if c + 2 < NCH:
                q_load(c + 2, [("sPE", mm4_done)])

            for bk in range(4):
                ev = self.stacked_tm.ap()[:, c, 64 + bk * 8:64 + bk * 8 + 8] \
                    .unsqueeze(2).broadcast_to([128, 8, P])
                E("vector", lambda e, bk=bk, ev=ev, b=b: e.tensor_mul(
                    y_full[b][:, bk * 512:(bk + 1) * 512]
                    .rearrange("p (h pp) -> p h pp", h=8),
                    self.pb[bk].ap().rearrange("p (h pp) -> p h pp", h=8),
                    ev), waits=[("sPE", mm4_done)] +
                    ([("sP", ptok_hist[c - 2])] if bk == 0 else []))
            ytt = t.now("sV")
            ytt_hist[c] = ytt
            padd = E("gpsimd", lambda e, b=b: e.tensor_add(
                y_full[b], y_full[b], ydl[b]),
                waits=[("sV", ytt), (f"yl{b}", yd_tok[c])])
            yd_war[b] = [("sP", padd)]
            if c + 2 < NCH:
                yd_load(c + 2)
            ptok = E("gpsimd", lambda e, b=b: e.tensor_mul(
                ysq[b], y_full[b], y_full[b]),
                waits=[("sV", red_hist[c - 2])])
            ptok_hist[c] = ptok
            if pend_mtok is not None:
                pend_mtok()
                pend_mtok = None
            if c + 1 < NCH:
                scan_step(c + 1)
            redt = E("vector", lambda e, b=b: e.tensor_reduce(
                rms, ysq[b].rearrange("p (h pp) -> p h pp", h=32),
                mybir.AxisListType.X, ALU.add),
                waits=[("sP", ptok), ("sA", rtok_hist[c - 1])])
            red_hist[c] = redt
            E("vector", lambda e: e.tensor_scalar(
                rms, rms, 1.0 / P, self.csts.ap()[:, 0:1], ALU.mult,
                ALU.add))
            vtok = t.now("sV")
            E("scalar", lambda e: e.activation(rms, rms, AF.Ln),
              waits=[("sV", vtok)])
            E("scalar", lambda e: e.activation(
                rms, rms, AF.Exp, scale=self.csts[:, 1:2]))
            rtok = t.now("sA")
            rtok_hist[c] = rtok
            rv = rms.unsqueeze(2).broadcast_to([128, 32, P])
            z1m = E("vector", lambda e, rv=rv, b=b: e.tensor_mul(
                z1.rearrange("p (h pp) -> p h pp", h=32),
                y_full[b].rearrange("p (h pp) -> p h pp", h=32), rv),
                waits=[("sA", rtok), ("sX", ztok_prev)])
            z1tok_hist[c] = z1m
            ztok = E("sync", lambda e: e.dma_start_transpose(z1T, z1),
                     waits=[("sV", z1m), ("sV", mtok_prev)], inc="sX",
                     inc_amt=16)
            ztok_prev = ztok
            for hv in range(2):
                sgtok = E("sync", lambda e, c=c, hv=hv: e.dma_start(
                    out=sgg[:, hv * 8:(hv + 1) * 8, :],
                    in_=self.sggD[hv * 8:(hv + 1) * 8, :,
                                  c * 128:(c + 1) * 128]
                    .transpose([1, 0, 2])),
                    waits=(self.sgw_toks + sg_war) if hv == 0 else (),
                    inc=f"sg{b}", inc_amt=16)

            def mk_mtok(c=c, b=b, ztok=ztok, sgtok=sgtok):
                nonlocal mtok_prev, sg_war
                mtok = E("vector", lambda e, c=c: e.tensor_mul(
                    self.z_fm[:, :, c * 128:(c + 1) * 128], z1T, sgg),
                    waits=[("sX", ztok), (f"sg{b}", sgtok)])
                mtok_prev = mtok
                sg_war = [("sV", mtok)]
            pend_mtok = mk_mtok
        pend_mtok()
        self.z_done = t.now("sV")

    def s7(self):
        E, t = self.tape.emit, self.tape
        t.stage_barrier()
        beta32 = self.carve(self.A3, 0, [128, 2048], F32)
        beta = self.carve(self.A3, 8192, [128, 2048], BF16)
        rawt = [self.carve(self.A3, 12288, [128, 512], BF16),
                self.carve(self.A3, 13312, [128, 512], BF16)]
        s1p = [self.carve(self.A3, 14336, [128, 512], BF16),
               self.carve(self.A3, 15360, [128, 512], BF16)]
        s1tg = self.carve(self.A1, 32768, [128, 4, 2048], BF16)
        xrl = [self.carve(self.A1, 49152, [128, 2048], F32),
               self.carve(self.A1, 57344, [128, 2048], F32)]
        outt = [self.carve(self.A1, 0, [128, 2048], F32),
                self.carve(self.A1, 8192, [128, 2048], F32)]

        wos = [self.carve(self.A4, 0, [128, NDC, 128], BF16),
               self.carve(self.A4, 4096, [128, NDC, 128], BF16)]
        wo4 = self.woT.rearrange("(dc p) f -> dc p f", p=128)
        wo_war = {0: [], 1: []}

        E("sync", lambda e: e.dma_start(
            out=beta32, in_=self.betaD.rearrange("(o a) -> o a", o=1)
            .broadcast_to([128, 2048])),
            waits=[self.btw_tok], inc="btr", inc_amt=16)
        btok = E("vector", lambda e: e.tensor_copy(beta, beta32),
                 waits=[("btr", 16), ("sV", self.z_done)])

        raw_war = {0: [], 1: []}
        s1_war = {0: [], 1: []}
        x_war = {0: [], 1: []}
        out_war = {0: [], 1: []}
        bank_free = {i: [] for i in range(6)}
        banks6 = [self.pb[i].ap() for i in range(6)]

        for tg in range(4):
            tptoks = []
            for fc in range(NDC):
                wb = fc % 2
                wtok = E("gpsimd", lambda e, fc=fc, wb=wb: e.dma_start(
                    out=wos[wb], in_=wo4[:, :, fc * 128:(fc + 1) * 128]
                    .transpose([1, 0, 2])),
                    waits=wo_war[wb], inc=f"wo{wb}", inc_amt=16)
                bkid = fc % 6
                bank = banks6[bkid]
                for dc in range(NDC):
                    w = []
                    if dc == 0:
                        w = [(f"wo{wb}", wtok), ("sV", self.z_done)] + \
                            bank_free[bkid]
                    E("tensor", lambda e, fc=fc, dc=dc, tg=tg, bank=bank,
                        wb=wb: e.matmul(
                        bank, wos[wb][:, dc, :],
                        self.z_fm[:, dc, tg * 512:(tg + 1) * 512],
                        start=(dc == 0), stop=(dc == NDC - 1)),
                        waits=w, inc="sPE")
                wo_war[wb] = [("sPE", t.now("sPE"))]
                wo_done = t.now("sPE")
                rb = fc % 2
                rtok = E("sync", lambda e, fc=fc, tg=tg, rb=rb: e.dma_start(
                    out=rawt[rb], in_=self.rawD[fc, :,
                                                tg * 512:(tg + 1) * 512]),
                    waits=self.rdw_toks + raw_war[rb], inc=f"rr{rb}",
                    inc_amt=16)
                E("vector", lambda e, rb=rb, tg=tg: e.tensor_mul(
                    s1p[rb], rawt[rb],
                    beta[:, tg * 512:(tg + 1) * 512]),
                    waits=[(f"rr{rb}", rtok), ("sV", btok)] + s1_war[rb])
                vtok = E("vector", lambda e, rb=rb, bank=bank: e.tensor_add(
                    s1p[rb], s1p[rb], bank), waits=[("sPE", wo_done)])
                raw_war[rb] = [("sV", vtok)]
                bank_free[bkid] = [("sV", vtok)]
                ttok = E("sync", lambda e, rb=rb, fc=fc: e.dma_start_transpose(
                    s1tg[:, :, fc * 128:(fc + 1) * 128], s1p[rb]),
                    waits=[("sV", vtok)], inc="sX", inc_amt=16)
                s1_war[rb] = [("sX", ttok)]
                tptoks.append(ttok)
            tp_done = tptoks[-1]
            for ti in range(4):
                tt = tg * 4 + ti
                ob = ti % 2
                xtok = E("sync", lambda e, tt=tt, ob=ob: e.dma_start(
                    out=xrl[ob], in_=self.x_in[1 + tt * 128:
                                               1 + (tt + 1) * 128, :]),
                    waits=x_war[ob], inc=f"xr2{ob}", inc_amt=16)
                otok = E("vector", lambda e, ti=ti, ob=ob: e.tensor_add(
                    outt[ob], s1tg[:, ti, :], xrl[ob]),
                    waits=[("sX", tp_done), (f"xr2{ob}", xtok)] +
                    out_war[ob])
                x_war[ob] = [("sV", otok)]
                ytok = E("sync", lambda e, tt=tt, ob=ob: e.dma_start(
                    out=self.y_out[tt * 128:(tt + 1) * 128, :],
                    in_=outt[ob]),
                    waits=[("sV", otok)], inc=f"yw{ob}", inc_amt=16)
                out_war[ob] = [(f"yw{ob}", ytok)]
            s1_war[0] = s1_war[0] + [("sV", t.now("sV"))]
            s1_war[1] = s1_war[1] + [("sV", t.now("sV"))]


    def drain_all(self):
        E, t = self.tape.emit, self.tape
        waits = [(n, t.now(n)) for n in t._names if t.now(n) > 0]
        def fin(e):
            return e.nop()
        E("sync", fin, waits=waits, dma=True)

    def finalize(self):
        nc, tape = self.nc, self.tape
        for s in tape._names:
            h = self.stack.enter_context(nc.semaphore(s))
            tape.handles[s] = h
        blk = self.stack.enter_context(nc.Block())

        def mk(eng_name):
            def body(eng):
                for fn in tape.items[eng_name]:
                    fn(eng)
            return body

        blk.sync(mk("sync"))
        blk.scalar(mk("scalar"))
        blk.tensor(mk("tensor"))
        blk.vector(mk("vector"))
        blk.gpsimd(mk("gpsimd"))
        self.stack.close()
        return nc


def host_inputs(full):
    x = np.asarray(full["L0_attn_hidden_states"], np.float32).reshape(16384, D)
    W = np.asarray(full["L0_attn_W_vkqgdt"], np.float32)
    conv_w = np.asarray(full["L0_attn_conv_w"], np.float32)
    conv_b = np.asarray(full["L0_attn_conv_b"], np.float32)
    dt_bias = np.asarray(full["L0_attn_dt_bias"], np.float32)
    alog = np.asarray(full["L0_attn_A_log_bias"], np.float32)
    gnorm = np.asarray(full["L0_attn_gnorm_w"], np.float32)
    Wo = np.asarray(full["L0_attn_W_o"], np.float32)
    wup = np.asarray(full["L0_ffn_w_up"], np.float32)
    wdn = np.asarray(full["L0_ffn_w_down"], np.float32)

    wT = np.ascontiguousarray(W.T)
    woT = np.ascontiguousarray(Wo.T)
    wupT = np.ascontiguousarray(wup.T)
    wdnT = np.ascontiguousarray(wdn.T)
    w0c = np.ascontiguousarray(conv_w[:, 0].reshape(CONVC, 128).T)
    w1c = np.ascontiguousarray(conv_w[:, 1].reshape(CONVC, 128).T)
    wbc = np.ascontiguousarray(conv_b.reshape(CONVC, 128).T)
    gn = np.tile(gnorm, 2).reshape(128, 1)
    dtb = dt_bias.reshape(H, 1)
    alogc = alog.reshape(H, 1)

    s_ = np.arange(128)[:, None]
    l_ = np.arange(128)[None, :]
    m = np.where(l_ < s_, np.float32(BIGNEG), np.float32(0.0))
    ltbigT = np.ascontiguousarray(np.tile(m, (1, H)))

    hm = np.ones((1, 1), np.float32)
    while hm.shape[0] < 128:
        hm = np.block([[hm, hm], [hm, -hm]])
    h128 = hm.astype(ml_dtypes.bfloat16)
    ident = np.eye(128, dtype=np.float32)
    identb = np.eye(128, dtype=ml_dtypes.bfloat16)

    shared = dict(wT=wT, woT=woT, w0c=w0c, w1c=w1c,
                  wbc=wbc, dtb=dtb, alog=alogc, gnormc=gn, ltbigT=ltbigT,
                  h128=h128, ident=ident, identb=identb)
    if _FAKE_CC:
        shared.update(wupT=wupT, wdnT=wdnT)

    maps = []
    for core in range(8):
        qtr = core % 4
        g0 = core * T
        x_sl = np.empty((T + 1, D), np.float32)
        x_sl[1:] = x[g0:g0 + T]
        x_sl[0] = 0.0 if qtr == 0 else x[g0 - 1]
        sel = np.zeros((1, 8), np.float32)
        em = np.zeros((8, 8), np.float32)
        grp = core // 4
        for k in range(grp * 4, grp * 4 + qtr):
            sel[0, k] = 1.0
            for mth in range(k + 1, grp * 4 + qtr):
                em[k, mth] = 1.0
        m_c = dict(shared, x_in=np.ascontiguousarray(x_sl),
                   sel8=sel, em8=em.reshape(1, 64),
                   om8=(1.0 - em).reshape(1, 64))
        if not _FAKE_CC:
            m_c["wupP"] = np.ascontiguousarray(
                wupT[core * 256:(core + 1) * 256, :])
            m_c["wdnP"] = np.ascontiguousarray(
                wdnT[core * 512:(core + 1) * 512, :])
        maps.append(m_c)
    return maps


def kernel(**inputs):
    mk = MK()
    mk.s0()
    mk.s1()
    mk.s15()
    mk.s2()
    mk.s4()
    mk.s5()
    mk.s6()
    mk.s7()
    mk.drain_all()
    nc = mk.finalize()
    maps = host_inputs(inputs)
    res = run_bass_kernel_spmd(nc, maps, core_ids=list(range(8)))
    out = np.concatenate([res.results[c]["y_out"] for c in range(8)], axis=0)
    return out.reshape(2, 8192, D).astype(np.float32)


# version 16
# speedup vs baseline: 1.0178x; 1.0178x over previous
import contextlib

import numpy as np
import ml_dtypes

import concourse.bass as bass
import concourse.mybir as mybir
from concourse.bass_utils import run_bass_kernel_spmd

F32 = mybir.dt.float32
BF16 = mybir.dt.bfloat16
INT8 = mybir.dt.int8
AF = mybir.ActivationFunctionType
ALU = mybir.AluOpType

MAGIC = 12582912.0
BIGNEG = -1.0e9
EPS = 1e-5

T = 2048
NTT = 16
NCH = 16
D = 2048
NDC = 16
H = 32
P = 64
PROJ = 5152
NFC = 41
CONVC = 24
F2 = 4096
NF2C = 32
NQ = 4
TUPH = 256 * 4096
TDNH = 512 * 2048

ENGINES = ("sync", "scalar", "tensor", "vector", "gpsimd")
_FAKE_CC = False
_SKIP_CC = False
_S6_LIMIT = 9


class Tape:

    def __init__(self):
        self.items = {e: [] for e in ENGINES}
        self.counts = {}
        self.last_wait = {e: {} for e in ENGINES}
        self.handles = {}
        self._names = []
        self.pending_barrier = {e: [] for e in ENGINES}
        for s in ("sX", "sPE", "sV", "sA", "sP", "sCC"):
            self.sem(s)

    def sem(self, name):
        if name not in self.counts:
            self.counts[name] = 0
            self._names.append(name)
        return name

    def stage_barrier(self):
        bl = [(n, v) for n, v in self.counts.items() if v > 0]
        self.pending_barrier = {e: list(bl) for e in ENGINES}

    ENG_SEM = {"scalar": "sA", "vector": "sV", "gpsimd": "sP"}

    def emit(self, eng, fn, waits=(), inc=None, inc_amt=1, dma=None):
        if dma is None:
            dma = (inc_amt == 16)
        waits = list(waits)
        if self.pending_barrier.get(eng):
            waits = self.pending_barrier[eng] + waits
            self.pending_barrier[eng] = []
        if inc == "sX":
            waits.append(("sX", self.counts.get("sX", 0)))
        if not dma and eng in self.ENG_SEM:
            own = self.ENG_SEM[eng]
            waits.append((own, self.counts.get(own, 0)))
            if inc is None:
                inc = own
                inc_amt = 1
        lw = self.last_wait[eng]
        rw = []
        for s, v in waits:
            if v > lw.get(s, 0):
                lw[s] = v
                rw.append((s, v))
        if inc is not None:
            self.sem(inc)
            self.counts[inc] += inc_amt

        def run(e, _fn=fn, _w=rw, _inc=inc, _amt=inc_amt):
            for s, v in _w:
                e.wait_ge(self.handles[s], v)
            ins = _fn(e)
            if _inc is not None:
                ins.then_inc(self.handles[_inc], _amt)
        self.items[eng].append(run)
        return self.counts[inc] if inc is not None else None

    def now(self, sem):
        return self.counts.get(sem, 0)


class MK:

    def __init__(self):
        self.nc = bass.Bass("TRN2", target_bir_lowering=False)
        self.tape = Tape()
        self.stack = contextlib.ExitStack()
        self._declare_dram()
        self._alloc_sbuf()

    def _declare_dram(self):
        nc = self.nc

        def din(name, shape, dt=F32):
            return nc.dram_tensor(name, shape, dt, kind="ExternalInput").ap()

        self.x_in = din("x_in", [T + 1, D])
        self.wT = din("wT", [D, PROJ])
        self.woT = din("woT", [D, D])
        if _FAKE_CC:
            self.wupT = din("wupT", [D, F2])
            self.wdnT = din("wdnT", [F2, D])
        else:
            self.wupP = din("wupP", [256, F2])
            self.wdnP = din("wdnP", [512, D])
        self.w0c = din("w0c", [128, CONVC])
        self.w1c = din("w1c", [128, CONVC])
        self.wbc = din("wbc", [128, CONVC])
        self.dtb = din("dtb", [H, 1])
        self.alog = din("alog", [H, 1])
        self.gnormc = din("gnormc", [128, 1])
        self.ltbigT = din("ltbigT", [128, H * 128])
        self.h128 = din("h128", [128, 128], BF16)
        self.ident = din("ident", [128, 128])
        self.identb = din("identb", [128, 128], BF16)
        self.sel8 = din("sel8", [1, 8])
        self.em8 = din("em8", [1, 64])
        self.om8 = din("om8", [1, 64])
        self.y_out = nc.dram_tensor("y_out", [T, D], F32,
                                    kind="ExternalOutput").ap()

        def dscr(name, shape, dt=F32):
            return nc.dram_tensor(name, shape, dt, kind="Internal").ap()

        self.xnD = dscr("xnD", [NDC, 128, T], BF16)
        self.qD = dscr("qD", [16, 128, T], BF16)
        self.sggD = dscr("sggD", [16, 128, T], BF16)
        self.ydD = dscr("ydD", [NCH, 128, H * P], BF16)
        self.rowD = dscr("rowD", [NCH, H * 128])
        self.ddT = dscr("ddT", [NCH, 2048])
        self.bndD = dscr("bndD", [D], BF16)
        self.drowD = dscr("drowD", [32])
        self.mrowD = dscr("mrowD", [8, 2048])
        self.ternPinU = dscr("ternPinU", [TUPH], BF16)
        self.ternPoutU = dscr("ternPoutU", [8 * TUPH], BF16)
        self.ternPinD = dscr("ternPinD", [TDNH], BF16)
        self.ternPoutD = dscr("ternPoutD", [8 * TDNH], BF16)
        self.ccSclIn = dscr("ccSclIn", [2])
        self.ccSclOut = dscr("ccSclOut", [16])
        self.sclbD = dscr("sclbD", [256])
        self.swD = dscr("swD", [2])
        self.rawD = dscr("rawD", [NDC, 128, T], BF16)
        self.betaD = dscr("betaD", [T])
        self.ccinD = dscr("ccinD", [128, 1056])
        self.ccoutD = dscr("ccoutD", [8 * 128, 1056])

    def _alloc_sbuf(self):
        nc, st = self.nc, self.stack

        def sbuf(name, shape, dt):
            return st.enter_context(nc.sbuf_tensor(name, shape, dt))

        self.A1 = sbuf("A1", [128, 32768], BF16)
        self.A2 = sbuf("A2", [128, 32768], BF16)
        self.A3 = sbuf("A3", [128, 12288], BF16)
        self.A4 = sbuf("A4", [128, 16384], BF16)

        self.stacked_tm = sbuf("stacked_tm", [128, NCH, 96], F32)
        self.w0_sb = sbuf("w0_sb", [128, CONVC], F32)
        self.w1_sb = sbuf("w1_sb", [128, CONVC], F32)
        self.wb_sb = sbuf("wb_sb", [128, CONVC], F32)
        self.gn_sb = sbuf("gn_sb", [128, 1], F32)
        self.dtb_sb = sbuf("dtb_sb", [H, 1], F32)
        self.alog_sb = sbuf("alog_sb", [H, 1], F32)
        self.acol_sb = sbuf("acol_sb", [H, 1], F32)
        self.h128_sb = sbuf("h128_sb", [128, 128], BF16)
        self.id_sb = sbuf("id_sb", [128, 128], F32)
        self.idb_sb = sbuf("idb_sb", [128, 128], BF16)
        self.bndf_sb = sbuf("bndf_sb", [128, NDC], BF16)
        self.bndp_sb = sbuf("bndp_sb", [128, CONVC], F32)
        self.alpha_sb = sbuf("alpha_sb", [128, NTT], F32)
        self.sw_sb = sbuf("sw_sb", [128, 2], F32)
        self.sel_sb = sbuf("sel_sb", [1, 8], F32)
        self.em_sb = sbuf("em_sb", [1, 64], F32)
        self.om_sb = sbuf("om_sb", [1, 64], F32)
        self.dall_sb = sbuf("dall_sb", [1, 256], F32)
        self.mr_sb = sbuf("mr_sb", [1, 64], F32)
        self.mrow_sb = sbuf("mrow_sb", [1, 2048], F32)
        self.smallf = sbuf("smallf", [128, 16], F32)
        self.drow_sb = sbuf("drow_sb", [1, 32], F32)
        self.drep_sb = sbuf("drep_sb", [128, 32], F32)
        self.csts = sbuf("csts", [128, 4], F32)
        self.smallk = sbuf("smallk", [128, 3072], BF16)

        def psum(name, shape):
            return st.enter_context(nc.psum_tensor(name, shape, F32))

        self.pb = [psum(f"pb{i}", [128, 512]) for i in range(6)]
        self.pst = psum("pst", [128, 1024])

    @staticmethod
    def qhalf(h):
        return (h >> 2) & 1

    @staticmethod
    def qslot(h):
        return ((h >> 3) << 2) + (h & 3)

    def carve(self, arena, byte_off, shape, dt):
        esz = 4 if dt == F32 else 2
        ap = arena.ap().bitcast(dt) if dt != BF16 else arena.ap()
        n = 1
        for s in shape[1:]:
            n *= s
        assert byte_off % esz == 0
        start = byte_off // esz
        v = ap[0:shape[0], start:start + n]
        if len(shape) == 3:
            v = v.rearrange("p (a b) -> p a b", a=shape[1], b=shape[2])
        elif len(shape) == 4:
            v = v.rearrange("p (a b c) -> p a b c", a=shape[1], b=shape[2],
                            c=shape[3])
        return v

    def s0(self):
        E, t = self.tape.emit, self.tape

        consts = [(self.w0_sb, self.w0c), (self.w1_sb, self.w1c),
                  (self.wb_sb, self.wbc), (self.gn_sb, self.gnormc),
                  (self.dtb_sb, self.dtb), (self.alog_sb, self.alog),
                  (self.h128_sb, self.h128), (self.id_sb, self.ident),
                  (self.idb_sb, self.identb), (self.sel_sb, self.sel8),
                  (self.em_sb, self.em8), (self.om_sb, self.om8)]
        for dst, src in consts:
            E("sync", lambda e, d=dst, s=src: e.dma_start(out=d[:], in_=s),
              inc="cst", inc_amt=16)
        self.cst_tok = ("cst", t.now("cst"))

        E("vector", lambda e: e.memset(self.csts[:, 0:1], EPS), inc="sV")
        E("vector", lambda e: e.memset(self.csts[:, 1:2], -0.5), inc="sV")
        E("vector", lambda e: e.memset(self.csts[:, 2:3], MAGIC), inc="sV")
        E("vector", lambda e: e.memset(self.csts[:, 3:4], -MAGIC), inc="sV")

        xcur = [self.carve(self.A2, 40960, [128, D], F32),
                self.carve(self.A2, 49152, [128, D], F32)]
        scr0 = self.carve(self.A2, 57344, [128, D], F32)
        xn_tm = [self.carve(self.A2, 8192, [128, D], BF16),
                 self.carve(self.A2, 12288, [128, D], BF16)]
        self.xn_fm = self.carve(self.A1, 0, [128, NDC, T], BF16)
        stat = self.carve(self.A3, 10240, [128, 8], F32)

        def ln_body(part, xv, xnv, first_waits):
            E("scalar", lambda e: e.activation(
                scr0[0:part, :], xv, AF.Identity,
                accum_out=stat[0:part, 0:1]), waits=first_waits, inc="sA")
            E("scalar", lambda e: e.activation(
                scr0[0:part, :], xv, AF.Square,
                accum_out=stat[0:part, 1:2]), inc="sA")
            aw = t.now("sA")
            E("vector", lambda e: e.tensor_scalar(
                stat[0:part, 2:3], stat[0:part, 0:1], 1.0 / D, None,
                ALU.mult), waits=[("sA", aw)], inc="sV")
            E("vector", lambda e: e.tensor_mul(
                stat[0:part, 6:7], stat[0:part, 2:3], stat[0:part, 2:3]),
                inc="sV")
            E("vector", lambda e: e.tensor_scalar(
                stat[0:part, 3:4], stat[0:part, 1:2], 1.0 / D, None,
                ALU.mult), inc="sV")
            E("vector", lambda e: e.tensor_sub(
                stat[0:part, 3:4], stat[0:part, 3:4], stat[0:part, 6:7]),
                inc="sV")
            vw = t.now("sV")
            E("scalar", lambda e: e.activation(
                stat[0:part, 5:6], stat[0:part, 3:4], AF.Ln,
                bias=self.csts[0:part, 0:1]),
                waits=[("sV", vw)], inc="sA")
            E("scalar", lambda e: e.activation(
                stat[0:part, 4:5], stat[0:part, 5:6], AF.Exp,
                scale=self.csts[0:part, 1:2]),
                inc="sA")
            aw2 = t.now("sA")
            return E("vector", lambda e: e.tensor_scalar(
                xnv, xv, stat[0:part, 2:3], stat[0:part, 4:5],
                ALU.subtract, ALU.mult), waits=[("sA", aw2)], inc="sV")

        self._ln_body = ln_body
        xbnd = self.carve(self.A3, 0, [1, D], F32)
        xbnd_n = self.carve(self.A3, 8192, [1, D], BF16)
        E("sync", lambda e: e.dma_start(out=xbnd, in_=self.x_in[0:1, :]),
          inc="bxl", inc_amt=16)
        vb = ln_body(1, xbnd, xbnd_n, [("bxl", 16)])
        E("sync", lambda e: e.dma_start(out=self.bndD[:], in_=xbnd_n),
          waits=[("sV", vb)], inc="bnw", inc_amt=16)
        bnd16 = self.carve(self.A3, 4096, [16, 128], BF16)
        E("sync", lambda e: e.dma_start(
            out=bnd16, in_=self.bndD.rearrange("(a b) -> a b", b=128)),
            waits=[("bnw", 16)], inc="bnr", inc_amt=16)
        pt = self.pb[0].ap().bitcast(BF16)[:, 0:16]
        E("tensor", lambda e: e.transpose(
            pt, bnd16, self.idb_sb[0:16, 0:16]),
            waits=[("bnr", 16), self.cst_tok], inc="sPE")
        pw = t.now("sPE")
        E("vector", lambda e: e.tensor_copy(self.bndf_sb[:], pt),
            waits=[("sPE", pw)], inc="sV")
        self.bndf_ready = t.now("sV")

        xsem = ["xld0", "xld1"]
        xn_done = {}
        tp_done = {}
        for tt in range(NTT):
            b = tt % 2
            w = []
            if tt >= 2:
                w = [("sV", xn_done[tt - 2]), ("sX", tp_done[tt - 2])]
            E("sync", lambda e, b=b, r=1 + tt * 128: e.dma_start(
                out=xcur[b], in_=self.x_in[r:r + 128, :]),
                waits=w, inc=xsem[b], inc_amt=16)
            qw = (xsem[b], t.now(xsem[b]))
            v = ln_body(128, xcur[b], xn_tm[b], [qw])
            xn_done[tt] = v
            tp_done[tt] = E("sync", lambda e, b=b, tt=tt:
                            e.dma_start_transpose(
                self.xn_fm[:, :, tt * 128:(tt + 1) * 128], xn_tm[b]),
                waits=[("sV", v)], inc="sX", inc_amt=16)
        self.xn_ready = t.now("sX")
        self.tp_done = tp_done
        self.xld_toks = [("xld0", t.now("xld0")), ("xld1", t.now("xld1"))]

        for c in range(NDC):
            E("sync", lambda e, c=c: e.dma_start(
                out=self.xnD[c], in_=self.xn_fm[:, c, :]),
                waits=[("sX", self.xn_ready)], inc="xnw", inc_amt=16)
        self.xnw_tok = ("xnw", t.now("xnw"))


    def s1(self):
        E, t = self.tape.emit, self.tape

        wtile = [self.carve(self.A2, 0, [128, NDC, 128], BF16),
                 self.carve(self.A2, 4096, [128, NDC, 128], BF16)]
        self.kfm = self.carve(self.A2, 8192, [128, 4, T], BF16)
        self.kT = self.carve(self.A2, 24576, [128, NCH, 512], BF16)
        self.vT = self.carve(self.A2, 40960, [128, NCH, 512], BF16)
        out_bf = [self.carve(self.A2, 57344, [128, 512], BF16),
                  self.carve(self.A2, 58368, [128, 512], BF16),
                  self.carve(self.A2, 59392, [128, 512], BF16),
                  self.carve(self.A2, 60416, [128, 512], BF16)]
        conv_t1 = [self.carve(self.A3, 0, [128, 512], F32),
                   self.carve(self.A3, 2048, [128, 512], F32)]
        conv_o = [self.carve(self.A3, 4096, [128, 512], F32),
                  self.carve(self.A3, 6144, [128, 512], F32)]
        self.dt_pre = self.carve(self.A1, 0, [32, T], F32)
        self._conv_state = {"t1_war": {0: 0, 1: 0}, "co_war": {0: 0, 1: 0},
                            "ob_war": [("sV", 0)] * 4}

        wT4 = self.wT.rearrange("(dc p) f -> dc p f", p=128)
        wsem = ["wt0", "wt1"]

        def load_wtile(buf, fc, ncols, waits):
            for h in range(2):
                E("gpsimd",
                  lambda e, b=buf, fc=fc, h=h, ncols=ncols: e.dma_start(
                      out=wtile[b][:, h * 8:(h + 1) * 8, 0:ncols],
                      in_=wT4[h * 8:(h + 1) * 8, :,
                              fc * 128:fc * 128 + ncols]
                      .transpose([1, 0, 2])),
                  waits=waits if h == 0 else (),
                  inc=wsem[buf], inc_amt=16)
            return (wsem[buf], t.now(wsem[buf]))

        pbnd = self.pb[0].ap()
        ld = load_wtile(0, 0, 128, [("sV", self.bndf_ready)])
        for fc in range(CONVC):
            nxt = None
            if fc + 1 < CONVC:
                nxt = load_wtile((fc + 1) % 2, fc + 1, 128,
                                 [("sPE", t.now("sPE"))])
            for dc in range(NDC):
                E("tensor", lambda e, b=fc % 2, dc=dc, fc=fc: e.matmul(
                    pbnd[:, fc:fc + 1], wtile[b][:, dc, :],
                    self.bndf_sb[:, dc:dc + 1],
                    start=(dc == 0), stop=(dc == NDC - 1)),
                    waits=[ld] if dc == 0 else (), inc="sPE")
            if nxt is not None:
                ld = nxt
        pe_bnd = t.now("sPE")
        E("vector", lambda e: e.tensor_copy(
            self.bndp_sb[:], pbnd[:, 0:CONVC]),
            waits=[("sPE", pe_bnd)], inc="sV")
        self.bnd_evac = t.now("sV")

        groups = [[self.pb[0].ap(), self.pb[1].ap(), self.pb[2].ap(),
                   self.pb[3].ap()],
                  [self.pb[4].ap(), self.pb[5].ap(),
                   self.pst.ap()[:, 0:512], self.pst.ap()[:, 512:1024]]]

        def bank_ap(g, i):
            return groups[g][i]

        grp_free = [[("sV", self.bnd_evac)], [()]]
        grp_free[1] = []
        ld = load_wtile(0, 0, 128, [("sV", self.bnd_evac)])

        for fc in range(NFC):
            g = fc % 2
            buf = fc % 2
            ncols = 128 if fc < 40 else 32
            nxt = None
            if fc + 1 < NFC:
                ncols2 = 128 if fc + 1 < 40 else 32
                nxt = load_wtile((fc + 1) % 2, fc + 1, ncols2,
                                 [("sPE", t.now("sPE"))])
            for tg in range(4):
                for dc in range(NDC):
                    w = []
                    if dc == 0:
                        w = list(grp_free[g])
                        if tg == 0:
                            w.append(ld)
                        if fc == 0:
                            w.append(("sX", self.tp_done[tg * 4 + 3]))
                    E("tensor", lambda e, b=buf, dc=dc, tg=tg, g=g,
                        nc_=ncols: e.matmul(
                            bank_ap(g, tg)[0:nc_, :], wtile[b][:, dc, 0:nc_],
                            self.xn_fm[:, dc, tg * 512:(tg + 1) * 512],
                            start=(dc == 0), stop=(dc == NDC - 1)),
                        waits=w, inc="sPE")
            mm_done = t.now("sPE")
            if nxt is not None:
                ld = nxt

            if fc < CONVC:
                ep = self._conv_epilogue(fc, g, bank_ap, mm_done,
                                         conv_t1, conv_o, out_bf)
            elif fc < 40:
                ep = self._gate_epilogue(fc, g, bank_ap, mm_done, out_bf)
            else:
                ep = self._dt_epilogue(g, bank_ap, mm_done)
            grp_free[g] = ep

        self.qdw_toks = [(f"qdw{i}", t.now(f"qdw{i}")) for i in range(4)]
        self.sgw_toks = [(f"sgw{i}", t.now(f"sgw{i}")) for i in range(4)]
        self.proj_done_v = t.now("sV")

    def _conv_epilogue(self, fc, g, bank_ap, mm_done, conv_t1, conv_o,
                       out_bf):
        E, t = self.tape.emit, self.tape
        cs = self._conv_state
        w0 = self.w0_sb.ap()[:, fc:fc + 1]
        w1 = self.w1_sb.ap()[:, fc:fc + 1]
        wb = self.wb_sb.ap()[:, fc:fc + 1]
        for tg in range(4):
            bk = bank_ap(g, tg)
            t1 = conv_t1[tg % 2]
            co = conv_o[tg % 2]
            E("scalar", lambda e, bk=bk, t1=t1, w1=w1: e.activation(
                t1, bk, AF.Identity, scale=w1),
                waits=[("sPE", mm_done), ("sV", cs["t1_war"][tg % 2]),
                       self.cst_tok], inc="sA")
            aw = t.now("sA")
            E("vector", lambda e, bk=bk, t1=t1, co=co, w0=w0:
                e.scalar_tensor_tensor(
                    co[:, 1:512], bk[:, 0:511], w0, t1[:, 1:512],
                    ALU.mult, ALU.add),
                waits=[("sA", aw), ("sA", cs["co_war"][tg % 2])], inc="sV")
            if tg == 0:
                prev = self.bndp_sb.ap()[:, fc:fc + 1]
            else:
                prev = bank_ap(g, tg - 1)[:, 511:512]
            E("vector", lambda e, prev=prev, t1=t1, co=co, w0=w0:
                e.scalar_tensor_tensor(
                    co[:, 0:1], prev, w0, t1[:, 0:1], ALU.mult, ALU.add),
                inc="sV")
            cs["t1_war"][tg % 2] = t.now("sV")
            vw = t.now("sV")
            ob = out_bf[tg]
            E("scalar", lambda e, co=co, ob=ob, wb=wb: e.activation(
                ob, co, AF.Silu, bias=wb),
                waits=[("sV", vw), cs["ob_war"][tg]], inc="sA")
            cs["co_war"][tg % 2] = t.now("sA")
            sil = t.now("sA")
            if fc < 4:
                tok = E("sync", lambda e, ob=ob, fc=fc, tg=tg:
                        e.dma_start_transpose(
                    self.vT[:, tg * 4:(tg + 1) * 4,
                            fc * 128:(fc + 1) * 128], ob),
                    waits=[("sA", sil)], inc="sX", inc_amt=16)
                cs["ob_war"][tg] = ("sX", tok)
            elif fc < 8:
                E("vector", lambda e, ob=ob, fc=fc, tg=tg: e.tensor_copy(
                    self.kfm[:, fc - 4, tg * 512:(tg + 1) * 512], ob),
                    waits=[("sA", sil)], inc="sV")
                tok = E("sync", lambda e, ob=ob, fc=fc, tg=tg:
                        e.dma_start_transpose(
                    self.kT[:, tg * 4:(tg + 1) * 4,
                            (fc - 4) * 128:(fc - 3) * 128], ob),
                    waits=[("sA", sil)], inc="sX", inc_amt=16)
                cs["ob_war"][tg] = ("sX", tok)
            else:
                tok = E("sync", lambda e, ob=ob, fc=fc, tg=tg: e.dma_start(
                    out=self.qD[fc - 8, :, tg * 512:(tg + 1) * 512], in_=ob),
                    waits=[("sA", sil)], inc=f"qdw{tg}", inc_amt=16)
                cs["ob_war"][tg] = (f"qdw{tg}", tok)
        return [("sV", t.now("sV")), ("sA", t.now("sA"))]

    def _gate_epilogue(self, fc, g, bank_ap, mm_done, out_bf):
        E, t = self.tape.emit, self.tape
        cs = self._conv_state
        gc = fc - CONVC
        for tg in range(4):
            bk = bank_ap(g, tg)
            ob = out_bf[tg]
            E("scalar", lambda e, bk=bk, ob=ob: e.activation(
                ob, bk, AF.Silu),
                waits=[("sPE", mm_done), cs["ob_war"][tg]], inc="sA")
            aw = t.now("sA")
            E("vector", lambda e, ob=ob: e.tensor_scalar(
                ob, ob, self.gn_sb.ap()[:, 0:1], None, ALU.mult),
                waits=[("sA", aw)], inc="sV")
            vw = t.now("sV")
            tok = E("sync", lambda e, ob=ob, gc=gc, tg=tg: e.dma_start(
                out=self.sggD[gc, :, tg * 512:(tg + 1) * 512], in_=ob),
                waits=[("sV", vw)], inc=f"sgw{tg}", inc_amt=16)
            cs["ob_war"][tg] = (f"sgw{tg}", tok)
        return [("sV", t.now("sV")), ("sA", t.now("sA"))]

    def _dt_epilogue(self, g, bank_ap, mm_done):
        E, t = self.tape.emit, self.tape
        for tg in range(4):
            bk = bank_ap(g, tg)
            E("vector", lambda e, bk=bk, tg=tg: e.tensor_copy(
                self.dt_pre[:, tg * 512:(tg + 1) * 512], bk[0:32, :]),
                waits=[("sPE", mm_done), self.xnw_tok], inc="sV")
        return [("sV", t.now("sV"))]

    def s15(self):
        E, t = self.tape.emit, self.tape
        u = self.carve(self.A1, 8192, [32, T], F32)
        absu = self.carve(self.A1, 16384, [32, T], F32)
        dt = self.carve(self.A1, 24576, [32, T], F32)
        lndt = self.carve(self.A1, 32768, [32, T], F32)
        self.Acs = self.carve(self.A1, 40960, [32, T], F32)
        stacked = self.carve(self.A1, 49152, [96, T], F32)
        ones32 = self.carve(self.A3, 0, [32, 128], F32)
        dfm = self.carve(self.A3, 2048, [32, NCH], F32)
        ca = self.carve(self.A3, 4096, [32, NCH], F32)
        dtm = self.carve(self.A3, 6144, [NCH, 32], F32)
        drows = self.carve(self.A3, 8192, [NCH, 2048], F32)

        pv = self.proj_done_v
        E("vector", lambda e: e.memset(ones32, 1.0),
          waits=[self.xnw_tok], inc="sV")
        E("vector", lambda e: e.tensor_scalar(
            u, self.dt_pre, self.dtb_sb.ap(), None, ALU.add),
            waits=[("sV", pv)], inc="sV")
        vw = t.now("sV")
        E("scalar", lambda e: e.activation(absu, u, AF.Abs),
          waits=[("sV", vw)], inc="sA")
        E("scalar", lambda e: e.activation(absu, absu, AF.Exp, scale=-1.0),
          inc="sA")
        E("scalar", lambda e: e.activation(absu, absu, AF.Ln, bias=1.0),
          inc="sA")
        aw = t.now("sA")
        E("vector", lambda e: e.tensor_scalar(
            dt, u, 0.0, None, ALU.max), inc="sV")
        E("vector", lambda e: e.tensor_add(dt, dt, absu),
          waits=[("sA", aw)], inc="sV")
        vw = t.now("sV")
        E("scalar", lambda e: e.activation(lndt, dt, AF.Ln),
          waits=[("sV", vw)], inc="sA")
        E("scalar", lambda e: e.activation(
            self.acol_sb[:], self.alog_sb[:], AF.Exp), inc="sA")
        aw = t.now("sA")
        E("vector", lambda e: e.tensor_scalar(
            u, dt, self.acol_sb.ap(), -1.0, ALU.mult, ALU.mult),
            waits=[("sA", aw)], inc="sV")
        vw = t.now("sV")
        for c in range(NCH):
            E("vector", lambda e, c=c: e.tensor_tensor_scan(
                self.Acs[:, c * 128:(c + 1) * 128],
                u[:, c * 128:(c + 1) * 128], ones32, 0.0, ALU.add, ALU.mult),
                waits=[("sV", vw)], inc="sV")
        acs_done = t.now("sV")
        E("vector", lambda e: e.tensor_sub(stacked[0:32, :], lndt, self.Acs),
          waits=[("sV", acs_done)], inc="sV")
        asum = self.Acs.rearrange("p (c l) -> p c l", c=NCH)[:, :, 127]
        E("scalar", lambda e: e.activation(dfm, asum, AF.Exp),
          waits=[("sV", acs_done)], inc="sA")
        fb_tmp = self.carve(self.A1, 57344, [32, T], F32)
        for c in range(NCH):
            E("scalar", lambda e, c=c: e.activation(
                fb_tmp[:, c * 128:(c + 1) * 128],
                self.Acs[:, c * 128:(c + 1) * 128], AF.Exp, scale=-1.0,
                bias=self.Acs[:, c * 128 + 127:c * 128 + 128]),
                waits=[("sV", acs_done)], inc="sA")
        aw = t.now("sA")
        E("vector", lambda e: e.tensor_mul(fb_tmp, fb_tmp, dt),
            waits=[("sA", aw)], inc="sV")
        E("vector", lambda e: e.tensor_copy(stacked[32:64, :], fb_tmp),
            inc="sV")
        E("scalar", lambda e: e.activation(
            stacked[64:96, :], self.Acs, AF.Exp),
            waits=[("sV", acs_done)], inc="sA")
        st_done = (t.now("sV"), t.now("sA"))
        self._dt = dt
        self._stacked = stacked

        E("vector", lambda e: e.tensor_tensor_scan(
            ca, dfm, ones32[:, 0:NCH], 1.0, ALU.mult, ALU.mult), inc="sV")
        caw = t.now("sV")
        ptD = self.pb[0].ap()[:, 0:32]
        E("tensor", lambda e: e.transpose(
            ptD[0:1, 0:32], ca[:, 15:16], self.id_sb[0:32, 0:32]),
            waits=[("sV", caw)], inc="sPE")
        pw = t.now("sPE")
        E("vector", lambda e: e.tensor_copy(self.drow_sb[:], ptD[0:1, 0:32]),
          waits=[("sPE", pw)], inc="sV")
        dr = t.now("sV")
        drow_copied = dr
        E("sync", lambda e: e.dma_start(
            out=self.drowD.rearrange("(o a) -> o a", o=1), in_=self.drow_sb[:]),
            waits=[("sV", dr)], inc="drw", inc_amt=16)
        E("sync", lambda e: e.dma_start(
            out=self.drep_sb[:],
            in_=self.drowD.rearrange("(o a) -> o a", o=1).broadcast_to([128, 32])),
            waits=[("drw", 16)], inc="drr", inc_amt=16)
        self.drep_ready = ("drr", 16)

        ptd = self.pb[0].ap()[:, 64:96]
        E("tensor", lambda e: e.transpose(
            ptd[0:NCH, 0:32], dfm, self.id_sb[0:32, 0:32]),
            waits=[("sV", drow_copied)], inc="sPE")
        pw = t.now("sPE")
        E("vector", lambda e: e.tensor_copy(dtm, ptd[0:NCH, 0:32]),
          waits=[("sPE", pw)], inc="sV")
        dw = t.now("sV")
        dview = dtm.rearrange("c (kh2 b2 j) -> c b2 kh2 j", kh2=4, b2=2)
        E("vector", lambda e: e.tensor_copy(
            drows.rearrange("c (b2 kh2 j p) -> c b2 kh2 j p",
                            b2=2, kh2=4, j=4),
            dview.unsqueeze(4).broadcast_to([NCH, 2, 4, 4, P])),
            waits=[("sV", dw)], inc="sV")
        dw2 = t.now("sV")
        E("sync", lambda e: e.dma_start(out=self.ddT, in_=drows),
          waits=[("sV", dw2)], inc="ddw", inc_amt=16)
        self.ddT_ready = ("ddw", 16)

        for c in range(NCH):
            E("sync", lambda e, c=c: e.dma_start(
                out=self.rowD[c].rearrange("(h l) -> h l", l=128),
                in_=self.Acs[:, c * 128:(c + 1) * 128]),
                waits=[("sV", acs_done)], inc="row", inc_amt=16)
        self.rowD_ready = ("row", t.now("row"))

        pstk = self.pb[1].ap()[:, 0:96]
        for c in range(NCH):
            E("tensor", lambda e, c=c: e.transpose(
                pstk, stacked[:, c * 128:(c + 1) * 128],
                self.id_sb[0:96, 0:96]),
                waits=[("sV", st_done[0]), ("sA", st_done[1]),
                       ("sV", t.now("sV"))], inc="sPE")
            pw = t.now("sPE")
            E("vector", lambda e, c=c: e.tensor_copy(
                self.stacked_tm[:, c, :], pstk), waits=[("sPE", pw)],
                inc="sV")
        self.stacked_ready = t.now("sV")

    def s2(self):
        E, t = self.tape.emit, self.tape
        t.stage_barrier()

        Rt = [self.carve(self.A1, 0, [128, 4096], F32),
              self.carve(self.A1, 16384, [128, 4096], F32)]
        Et = [self.carve(self.A1, 32768, [128, 4096], BF16),
              self.carve(self.A1, 40960, [128, 4096], BF16)]
        Mt = [self.carve(self.A1, 49152, [128, 2048], BF16),
              self.carve(self.A1, 53248, [128, 2048], BF16)]
        ydt = self.carve(self.A1, 57344, [128, 2048], BF16)
        qbuf = [self.carve(self.A3, 0, [128, 32, 128], BF16),
                self.carve(self.A3, 8192, [128, 32, 128], BF16)]
        dmul = [self.carve(self.A3, 16384, [128, 1024], F32),
                self.carve(self.A3, 20480, [128, 1024], F32)]
        ktl2 = [self.carve(self.smallk, 128 * i, [128, 64], BF16)
                for i in range(16)]
        ktl_war = {i: 0 for i in range(16)}
        R = self.carve(self.smallk, 2048, [128, 1024], F32)
        self.staging = self.carve(self.A4, 0, [128, NCH, 1024], BF16)

        kv_ready = ("sX", self.xn_ready and t.now("sX"))
        qdw = self.qdw_toks

        rt_tok = {}
        ra_tok = {}
        qs_tok = {}
        dm_tok = {}
        rt_war = {0: [], 1: []}
        q_war = {0: [], 1: []}
        dm_war = {0: [], 1: []}
        e_war = {0: [], 1: []}
        m_war = {0: [], 1: []}

        def prefetch(c):
            b = c % 2
            rt_tok[c] = E("sync", lambda e, b=b: e.dma_start(
                out=Rt[b], in_=self.ltbigT.broadcast_to([128, 4096])
                if False else self.ltbigT),
                waits=[self.cst_tok] + rt_war[b], inc=f"rt{b}", inc_amt=16)
            for hv in range(2):
                ra_tok[c] = E("gpsimd", lambda e, b=b, c=c, hv=hv:
                              e.dma_start(
                    out=Rt[b][:, hv * 2048:(hv + 1) * 2048],
                    in_=self.rowD[c, hv * 2048:(hv + 1) * 2048]
                    .rearrange("(o a) -> o a", o=1)
                    .broadcast_to([128, 2048]),
                    accum_op=ALU.add),
                    waits=[(f"rt{b}", rt_tok[c]), self.rowD_ready]
                    if hv == 0 else (),
                    inc=f"ra{b}", inc_amt=16, dma=True)
            qv = self.qD \
                .rearrange("c (two p) t -> (c two) p t", two=2) \
                .rearrange("(kh2 half j) p t -> half kh2 j p t",
                           kh2=4, half=2, j=4)
            for hf in range(2):
                for kh2 in range(4):
                    h0 = kh2 * 8 + hf * 4
                    tok = E("sync", lambda e, b=b, c=c, hf=hf, kh2=kh2,
                            h0=h0, qv=qv: e.dma_start(
                        out=qbuf[b][hf * 64:(hf + 1) * 64, h0:h0 + 4, :],
                        in_=qv[hf, kh2, :, :, c * 128:(c + 1) * 128]
                        .transpose([1, 0, 2])),
                        waits=(qdw + q_war[b] +
                               [("sV", qz_init)]) if (hf == 0 and kh2 == 0)
                        else (), inc=f"qs{b}", inc_amt=16)
            qs_tok[c] = tok
            if c > 0:
                dm_tok[c] = E("sync", lambda e, b=b, c=c: e.dma_start(
                    out=dmul[b], in_=self.ddT[c]
                    .rearrange("(b2 rest) -> b2 rest", b2=2)
                    .unsqueeze(1).broadcast_to([2, 64, 1024])),
                    waits=[self.ddT_ready] + dm_war[b], inc=f"dm{b}",
                    inc_amt=16)

        E("vector", lambda e: e.memset(qbuf[0], 0.0))
        qz_init = E("vector", lambda e: e.memset(qbuf[1], 0.0))
        prefetch(0)
        prefetch(1)
        pst_free = []
        yd_free = []
        scan_tok = 0

        for c in range(NCH):
            b = c % 2
            vtoks = {}
            etoks = {}
            for hf2 in range(2):
                eng = "vector" if hf2 == 0 else "gpsimd"
                sem = "sV" if hf2 == 0 else "sP"
                ctv = self.stacked_tm.ap()[:, c, hf2 * 16:(hf2 + 1) * 16] \
                    .unsqueeze(2).broadcast_to([128, 16, 128])
                rv_ = Rt[b].rearrange("p (h l) -> p h l", h=32) \
                    [:, hf2 * 16:(hf2 + 1) * 16, :]
                vtoks[hf2] = E(eng, lambda e, ctv=ctv, rv_=rv_: e.tensor_add(
                    rv_, rv_, ctv),
                    waits=[(f"ra{b}", ra_tok[c]),
                           ("sV", self.stacked_ready)])
                etoks[hf2] = E("scalar", lambda e, b=b, hf2=hf2:
                               e.activation(
                    Et[b][:, hf2 * 2048:(hf2 + 1) * 2048],
                    Rt[b][:, hf2 * 2048:(hf2 + 1) * 2048], AF.Exp),
                    waits=[(sem, vtoks[hf2])] +
                    (e_war[b] if hf2 == 0 else []))
            etok = etoks[1]
            rt_war[b] = [("sA", etok)]

            mm1_done = {}
            mm3_done = {}
            st_written = []
            for half in range(2):
                for hh in range(16):
                    h = half * 16 + hh
                    kvh = h >> 2
                    lhs = self.kfm[:, kvh >> 1, c * 128:(c + 1) * 128]
                    rhs = qbuf[b][:, h, :]
                    bank = self.pb[hh // 4].ap()
                    w = []
                    if hh == 0:
                        w = [(f"qs{b}", qs_tok[c]), kv_ready,
                             ("sV", t.now("sV"))] + (pst_free if half == 0
                                                     else [])
                        if half == 1:
                            w = [("sV", t.now("sV"))]
                    E("tensor", lambda e, lhs=lhs, rhs=rhs, bank=bank, hh=hh:
                        e.matmul(bank[:, (hh % 4) * 128:(hh % 4 + 1) * 128],
                                 lhs, rhs, start=True, stop=True),
                        waits=w, inc="sPE")
                mm1_done[half] = t.now("sPE")

                ktoks = {}
                ksem = {}
                for hh in range(16):
                    h = half * 16 + hh
                    kvh = h >> 2
                    fb = self.stacked_tm.ap()[:, c, 32 + h:33 + h]
                    if hh % 2 == 0:
                        ktoks[hh] = E("vector", lambda e, hh=hh, kvh=kvh,
                                      fb=fb, c=c: e.tensor_scalar(
                            ktl2[hh], self.kT[:, c, kvh * 64:kvh * 64 + 64],
                            fb, None, ALU.mult),
                            waits=[("sV", self.stacked_ready), kv_ready,
                                   ("sPE", ktl_war[hh])])
                        ksem[hh] = "sV"
                    else:
                        ktoks[hh] = E("scalar", lambda e, hh=hh, kvh=kvh,
                                      fb=fb, c=c: e.activation(
                            ktl2[hh], self.kT[:, c, kvh * 64:kvh * 64 + 64],
                            AF.Identity, scale=fb),
                            waits=[("sV", self.stacked_ready), kv_ready,
                                   ("sPE", ktl_war[hh])])
                        ksem[hh] = "sA"

                for bk in range(4):
                    esl = Et[b].rearrange("p (h l) -> p h l", h=32)[
                        :, half * 16 + bk * 4:half * 16 + bk * 4 + 4, :]
                    E("vector", lambda e, bk=bk, esl=esl, b=b: e.tensor_mul(
                        Mt[b][:, bk * 512:(bk + 1) * 512],
                        self.pb[bk].ap().rearrange("p (h l) -> p h l", h=4),
                        esl),
                        waits=[("sPE", mm1_done[half]),
                               ("sA", etoks[half])] +
                        (m_war[b] if bk == 0 else []))
                mmul_done = t.now("sV")

                for hh in range(16):
                    h = half * 16 + hh
                    kvh = h >> 2
                    b2 = self.qhalf(h)
                    h2 = self.qslot(h)
                    dst = self.pst.ap()[b2 * 64:b2 * 64 + 64,
                                        h2 * 64:h2 * 64 + 64]
                    w = [(ksem[hh], ktoks[hh])]
                    if half == 0 and hh == 0:
                        w += pst_free
                    E("tensor", lambda e, hh=hh, kvh=kvh, dst=dst, c=c,
                        b2=b2: e.matmul(
                        dst, ktl2[hh], self.vT[:, c, kvh * 64:kvh * 64 + 64],
                        start=True, stop=True,
                        tile_position=(0, 64) if b2 else (0, 0)),
                        waits=w, inc="sPE")
                    mm3_done[half] = t.now("sPE")
                    ktl_war[hh] = mm3_done[half]

                    ydbank = self.pb[4 + (hh // 8)].ap()
                    w2 = []
                    if hh == 0:
                        w2 = [("sV", mmul_done)] + (yd_free if half == 0
                                                    else [])
                    E("tensor", lambda e, hh=hh, h=h, b=b, ydbank=ydbank, c=c:
                        e.matmul(
                        ydbank[:, (hh % 8) * 64:(hh % 8) * 64 + 64],
                        Mt[b][:, hh * 128:(hh + 1) * 128],
                        self.vT[:, c, (h >> 2) * 64:(h >> 2) * 64 + 64],
                        start=True, stop=True),
                        waits=w2, inc="sPE")
                mm2_done = t.now("sPE")
                m_war[b] = [("sPE", mm2_done)]

                for bk in range(2):
                    eng = "vector" if bk == 0 else "scalar"
                    fn = (lambda e, bk=bk, half=half: e.tensor_copy(
                        ydt[:, half * 1024 + bk * 512:
                            half * 1024 + (bk + 1) * 512],
                        self.pb[4 + bk].ap())) if bk == 0 else \
                        (lambda e, bk=bk, half=half: e.activation(
                            ydt[:, half * 1024 + bk * 512:
                                half * 1024 + (bk + 1) * 512],
                            self.pb[4 + bk].ap(), AF.Identity))
                    E(eng, fn, waits=[("sPE", mm2_done)] +
                      ([("ydw", t.now("ydw"))] if half == 0 else []))
                yd_free = [("sV", t.now("sV")), ("sA", t.now("sA"))]

            ydw_tok = E("sync", lambda e, c=c: e.dma_start(
                out=self.ydD[c], in_=ydt),
                waits=[("sV", t.now("sV")), ("sA", t.now("sA"))],
                inc="ydw", inc_amt=16)

            stok = E("scalar", lambda e, c=c: e.activation(
                self.staging[:, c, :], self.pst.ap(), AF.Identity),
                waits=[("sPE", mm3_done[1])])
            pst_free = [("sA", stok)]

            if c == 0:
                scan_tok = E("vector", lambda e: e.tensor_copy(
                    R, self.staging[:, 0, :]), waits=[("sA", stok)])
            else:
                E("vector", lambda e, b=b: e.tensor_mul(R, R, dmul[b]),
                  waits=[(f"dm{b}", dm_tok[c]), ("sA", stok)])
                scan_tok = E("vector", lambda e, c=c: e.tensor_add(
                    R, R, self.staging[:, c, :]))
            dm_war[b] = [("sV", scan_tok)]
            q_war[b] = [("sPE", mm1_done[1])]
            e_war[b] = [("sV", t.now("sV"))]

            if c + 2 < NCH:
                prefetch(c + 2)

        ccw1 = E("sync", lambda e: e.dma_start(
            out=self.ccinD[:, 0:1024], in_=R),
            waits=[("sV", scan_tok)], inc="ccw", inc_amt=16)
        E("sync", lambda e: e.dma_start(
            out=self.ccinD[:, 1024:1056], in_=self.drep_sb[:]),
            waits=[self.drep_ready], inc="ccw", inc_amt=16)
        if _SKIP_CC:
            pass
        elif _FAKE_CC:
            for r in range(8):
                E("gpsimd", lambda e, r=r: e.dma_start(
                    out=self.ccoutD[r * 128:(r + 1) * 128, :],
                    in_=self.ccinD),
                    waits=[("ccw", t.now("ccw"))], inc="sCC", inc_amt=16,
                    dma=True)
        else:
            E("gpsimd", lambda e: e.collective_compute(
                "AllGather", ALU.bypass, replica_groups=[list(range(8))],
                ins=[self.ccinD.opt()], outs=[self.ccoutD.opt()]),
                waits=[("ccw", t.now("ccw"))], inc="sCC", inc_amt=1,
                dma=True)
        self.scan1_R = R
        self.s2_dmul = dmul
        self.s2_dm_war = dm_war
        self.s2_qbuf = qbuf

    def s4(self):
        E, t = self.tape.emit, self.tape
        t.stage_barrier()

        xnq0 = self.carve(self.A2, 0, [128, NDC, 512], BF16)
        xr0 = E("sync", lambda e: e.dma_start(
            out=xnq0, in_=self.xnD[:, :, 0:512]
            .transpose([1, 0, 2])),
            waits=[self.xnw_tok], inc="xnr0", inc_amt=16)

        wslab = [self.carve(self.A1, 16384 * i, [128, 2048], F32)
                 for i in range(4)]
        tslab = [self.carve(self.A1, 16384 * i + 8192, [128, 2048], BF16)
                 for i in range(4)]
        abss = self.carve(self.A1, 12288, [128, 72], F32)
        sclr = self.carve(self.A1, 12576, [1, 288], F32)
        tscr = [self.carve(self.A3, 0, [128, 2048], F32),
                self.carve(self.A3, 8192, [128, 2048], F32)]

        ws_war = {i: [] for i in range(4)}
        if _FAKE_CC:
            inter = [("u", i >> 1, i & 1) for i in range(32)] + \
                    [("d", i, 0) for i in range(32)]
        else:
            inter = [("u", si >> 1, si & 1) for si in range(4)] + \
                    [("d", si, 0) for si in range(4)]

        tpoU = self.ternPoutU.rearrange("(r c) -> r c", c=TUPH)
        tpoD = self.ternPoutD.rearrange("(r c) -> r c", c=TDNH)
        tpi_u = self.ternPinU.rearrange(
            "(s p f) -> s p f", s=2, p=128, f=4096)
        tpi_d = self.ternPinD.rearrange(
            "(s p f) -> s p f", s=4, p=128, f=2048)

        def slab_src(j):
            m, i, hv = j
            if _FAKE_CC:
                if m == "u":
                    return self.wupT[i * 128:(i + 1) * 128,
                                     hv * 2048:(hv + 1) * 2048]
                return self.wdnT[i * 128:(i + 1) * 128, :]
            if m == "u":
                return self.wupP[i * 128:(i + 1) * 128,
                                 hv * 2048:(hv + 1) * 2048]
            return self.wdnP[i * 128:(i + 1) * 128, :]

        def slab_dst(j):
            m, i, hv = j
            if _FAKE_CC:
                if m == "u":
                    r, s = i >> 1, i & 1
                    return tpoU[r].rearrange(
                        "(s p f) -> s p f", s=2, p=128, f=4096)[
                        s, :, hv * 2048:(hv + 1) * 2048]
                r, s = i >> 2, i & 3
                return tpoD[r].rearrange(
                    "(s p f) -> s p f", s=4, p=128, f=2048)[s]
            if m == "u":
                return tpi_u[i][:, hv * 2048:(hv + 1) * 2048]
            return tpi_d[i]

        for n, j in enumerate(inter):
            b = n % 4
            tok = E("sync", lambda e, j=j, b=b: e.dma_start(
                out=wslab[b], in_=slab_src(j)),
                waits=ws_war[b], inc=f"aw{b}", inc_amt=16)
            atok = E("scalar", lambda e, b=b, n=n: e.activation(
                tscr[b % 2], wslab[b], AF.Abs,
                accum_out=abss[:, n:n + 1]),
                waits=[(f"aw{b}", tok)])
            ws_war[b] = [("sA", atok)]
        a_all = t.now("sA")
        nu = len(inter) // 2
        vu = E("vector", lambda e, nu=nu: e.tensor_reduce(
            abss[:, 64:65], abss[:, 0:nu],
            mybir.AxisListType.X, ALU.add), waits=[("sA", a_all)])
        vd = E("vector", lambda e, nu=nu: e.tensor_reduce(
            abss[:, 65:66], abss[:, nu:2 * nu],
            mybir.AxisListType.X, ALU.add))

        t1 = E("gpsimd", lambda e: e.dma_start(
            out=self.sclbD.rearrange("(p c) -> p c", c=2),
            in_=abss[:, 64:66]),
            waits=[("sV", t.now("sV"))], inc="auw", inc_amt=16, dma=True)
        t2 = E("gpsimd", lambda e: e.dma_start(
            out=sclr[0:1, 0:256].rearrange("o (c p) -> o c p", c=2),
            in_=self.sclbD.rearrange("(p c) -> c p", c=2).unsqueeze(0),
            allow_slow_non_contiguous=True),
            waits=[("auw", t1)], inc="aur", inc_amt=16, dma=True)
        rtok = E("vector", lambda e: e.tensor_reduce(
            sclr[0:1, 256:258],
            sclr[0:1, 0:256].rearrange("o (c p) -> o c p", c=2),
            mybir.AxisListType.X, ALU.add), waits=[("aur", t2)])
        t3 = E("gpsimd", lambda e: e.dma_start(
            out=self.ccSclIn.rearrange("(o a) -> o a", o=1),
            in_=sclr[0:1, 256:258]),
            waits=[("sV", rtok)], inc="scw", inc_amt=16, dma=True)
        if _FAKE_CC:
            for r in range(8):
                E("gpsimd", lambda e, r=r: e.dma_start(
                    out=self.ccSclOut[r * 2:(r + 1) * 2]
                    .rearrange("(o a) -> o a", o=1),
                    in_=self.ccSclIn.rearrange("(o a) -> o a", o=1)),
                    waits=[("scw", t3)] if r == 0 else [], inc="sCS",
                    inc_amt=16, dma=True)
            scs_wait = [("sCS", t.now("sCS"))]
        else:
            E("gpsimd", lambda e: e.collective_compute(
                "AllGather", ALU.bypass, replica_groups=[list(range(8))],
                ins=[self.ccSclIn.opt()], outs=[self.ccSclOut.opt()]),
                waits=[("scw", t3)], inc="sCS", inc_amt=1, dma=True)
            scs_wait = [("sCS", 1)]
        t4 = E("gpsimd", lambda e: e.dma_start(
            out=sclr[0:1, 258:274].rearrange("o (c r) -> o c r", c=2),
            in_=self.ccSclOut.rearrange("(r c) -> c r", c=2).unsqueeze(0),
            allow_slow_non_contiguous=True),
            waits=scs_wait, inc="scr", inc_amt=16, dma=True)
        stok = E("vector", lambda e: e.tensor_reduce(
            sclr[0:1, 274:276],
            sclr[0:1, 258:274].rearrange("o (c r) -> o c r", c=2),
            mybir.AxisListType.X, ALU.add), waits=[("scr", t4)])
        if _FAKE_CC:
            stok = E("vector", lambda e: e.tensor_scalar(
                sclr[0:1, 274:276], sclr[0:1, 274:276], 1.0 / 8.0, None,
                ALU.mult))
        t5 = E("gpsimd", lambda e: e.dma_start(
            out=self.swD.rearrange("(o a) -> o a", o=1),
            in_=sclr[0:1, 274:276]),
            waits=[("sV", stok)], inc="sws", inc_amt=16, dma=True)
        t6 = E("gpsimd", lambda e: e.dma_start(
            out=self.sw_sb[:, 0:2],
            in_=self.swD.rearrange("(o a) -> o a", o=1)
            .broadcast_to([128, 2])),
            waits=[("sws", t5)], inc="swb", inc_amt=16, dma=True)
        E("vector", lambda e: e.tensor_scalar(
            self.sw_sb[:, 0:2], self.sw_sb[:, 0:2], 1.0 / (D * F2), EPS,
            ALU.mult, ALU.max), waits=[("swb", t6)])
        su_tok = t.now("sV")
        inv_up = self.carve(self.A1, 13728, [128, 1], F32)
        inv_dn = self.carve(self.A1, 13792, [128, 1], F32)
        E("vector", lambda e: e.reciprocal(
            inv_up, self.sw_sb.ap()[:, 0:1]), waits=[("sV", su_tok)])
        E("vector", lambda e: e.reciprocal(
            inv_dn, self.sw_sb.ap()[:, 1:2]))
        inv_tok = t.now("sV")

        tscr_war = {0: [], 1: []}
        tsl_war = {i: [] for i in range(4)}
        for n, j in enumerate(inter):
            b = n % 4
            tok = E("sync", lambda e, j=j, b=b: e.dma_start(
                out=wslab[b], in_=slab_src(j)),
                waits=ws_war[b], inc=f"tw{b}", inc_amt=16)
            inv = inv_up if j[0] == "u" else inv_dn
            a1 = E("scalar", lambda e, b=b, inv=inv: e.activation(
                tscr[b % 2], wslab[b], AF.Identity, scale=inv,
                bias=self.csts[:, 2:3]),
                waits=[(f"tw{b}", tok), ("sV", inv_tok)] + tscr_war[b % 2])
            ws_war[b] = [("sA", a1)]
            mintok = E("vector", lambda e, b=b: e.tensor_scalar(
                tslab[b], tscr[b % 2], MAGIC, 1.0, ALU.subtract, ALU.min),
                waits=[("sA", a1)] + tsl_war[b])
            tscr_war[b % 2] = [("sV", mintok)]
            vtok = E("vector", lambda e, b=b: e.tensor_scalar(
                tslab[b], tslab[b], -1.0, None, ALU.max))
            wsem = ("tuw" if j[0] == "u" else "tdw") + str(b)
            wtok = E("gpsimd", lambda e, j=j, b=b: e.dma_start(
                out=slab_dst(j), in_=tslab[b]),
                waits=[("sV", vtok)], inc=wsem, inc_amt=16, dma=True)
            tsl_war[b] = [(wsem, wtok)]
        tuw_toks = [(f"tuw{b}", t.now(f"tuw{b}")) for b in range(4)]
        tdw_toks = [(f"tdw{b}", t.now(f"tdw{b}")) for b in range(4)]
        if _FAKE_CC:
            up_cache_wait = list(tuw_toks)
            dn_cache_wait = list(tdw_toks)
        else:
            E("gpsimd", lambda e: e.collective_compute(
                "AllGather", ALU.bypass, replica_groups=[list(range(8))],
                ins=[self.ternPinU.opt()], outs=[self.ternPoutU.opt()]),
                waits=tuw_toks, inc="sCU", inc_amt=1, dma=True)
            E("gpsimd", lambda e: e.collective_compute(
                "AllGather", ALU.bypass, replica_groups=[list(range(8))],
                ins=[self.ternPinD.opt()], outs=[self.ternPoutD.opt()]),
                waits=tdw_toks, inc="sCD", inc_amt=1, dma=True)
            up_cache_wait = [("sCU", 1)]
            dn_cache_wait = [("sCD", 1)]
        tern_dve_done = t.now("sV")
        tern_act_done = t.now("sA")
        E("vector", lambda e: e.tensor_scalar(
            self.sw_sb[:, 0:1], self.sw_sb[:, 0:1],
            1.0 / (127.0 * float(np.sqrt(2048.0))), None, ALU.mult))
        E("vector", lambda e: e.tensor_scalar(
            self.sw_sb[:, 1:2], self.sw_sb[:, 1:2], 1.0 / 127.0, None,
            ALU.mult))
        sw_ready = t.now("sV")
        tern_tok = [("sV", tern_dve_done), ("sA", tern_act_done)]

        xnq = self.carve(self.A2, 0, [128, NDC, 512], BF16)
        u1 = self.carve(self.A2, 16384, [128, 2048], BF16)
        pong1 = self.carve(self.A2, 20480, [128, 2048], BF16)
        y1 = self.carve(self.A2, 24576, [128, 2048], F32)
        xqf = self.carve(self.A2, 32768, [128, NDC, 512], BF16)
        terns = [self.carve(self.A2, 49152, [128, 16, 128], BF16),
                 self.carve(self.A2, 53248, [128, 16, 128], BF16)]
        xq1t = self.carve(self.A2, 57344, [128, 2048], BF16)
        ternsD = [self.carve(self.A2, 49152, [128, 32, 128], BF16),
                  self.carve(self.A2, 57344, [128, 32, 128], BF16)]
        xq2f = self.carve(self.A2, 0, [128, NF2C, 512], BF16)
        h2q = self.carve(self.A1, 0, [128, NF2C, 512], BF16)
        u2 = self.carve(self.A1, 32768, [128, 4096], BF16)
        pong2 = self.carve(self.A1, 40960, [128, 4096], BF16)
        y2h = self.carve(self.A1, 49152, [128, 2048], F32)
        xq2t = self.carve(self.A1, 57344, [128, 4096], BF16)
        rawq = [self.carve(self.A1, 0, [128, 512], BF16),
                self.carve(self.A1, 1024, [128, 512], BF16)]
        mcol = self.carve(self.A3, 16384, [128, 8], F32)
        betac = self.carve(self.A3, 16416, [128, 1], F32)

        banks6 = [self.pb[i].ap() for i in range(6)]
        banks8 = banks6 + [self.pst.ap()[:, 0:512],
                           self.pst.ap()[:, 512:1024]]
        bank_free = {i: [] for i in range(8)}

        xnr_war = []
        u1_war = list(tern_tok)
        pong1_war = list(tern_tok)
        u2_war = tuw_toks + tdw_toks + tern_tok
        pong2_war = tuw_toks + tdw_toks + list(tern_tok)
        tern_war = {0: [], 1: []}
        xqf_war = []
        xq2f_war = []
        raw_war = {0: list(tern_tok), 1: list(tern_tok)}
        y1_war = []
        y2_war = []
        xq1t_war = []
        xq2t_war = list(tern_tok) + tuw_toks + tdw_toks
        h2_war = list(tern_tok) + tuw_toks + tdw_toks

        wT_up4 = tpoU.rearrange(
            "r (s p f) -> r s p f", s=2, p=128, f=4096)
        wT_dn4 = tpoD.rearrange(
            "r (s p f) -> r s p f", s=4, p=128, f=2048)

        for q in range(NQ):
            xb = q % 2
            xq_toks = {}
            x2_toks = {}
            if q == 0:
                xr = xr0
            else:
                xr = E("sync", lambda e, q=q: e.dma_start(
                    out=xnq, in_=self.xnD[:, :, q * 512:(q + 1) * 512]
                    .transpose([1, 0, 2])),
                    waits=[self.xnw_tok] + xnr_war,
                    inc=f"xnr{xb}", inc_amt=16)
            xr_tok = (f"xnr{xb}", xr)

            for ti in range(4):
                tt = q * 4 + ti
                bu = ti % 2
                g0 = 0 if bu == 0 else 4
                for c in range(NDC):
                    w = [xr_tok]
                    if c == 0:
                        w = [xr_tok] + bank_free[g0] + bank_free[g0 + 1] + \
                            bank_free[g0 + 2] + bank_free[g0 + 3]
                    E("tensor", lambda e, c=c, ti=ti, g0=g0: e.matmul(
                        banks8[g0 + c % 4][:, (c // 4) * 128:
                                           (c // 4 + 1) * 128],
                        xnq[:, c, ti * 128:(ti + 1) * 128], self.h128_sb[:],
                        start=True, stop=True),
                        waits=w, inc="sPE")
                f1_done = t.now("sPE")
                for bk in range(4):
                    eng = "vector" if bk % 2 == 0 else "scalar"
                    sem = "sV" if bk % 2 == 0 else "sA"
                    fn = (lambda e, bk=bk, g0=g0: e.tensor_copy(
                        u1.rearrange("p (c l) -> p c l", c=16)
                        [:, bk::4, :], banks8[g0 + bk]
                        .rearrange("p (s l) -> p s l", s=4))) \
                        if bk % 2 == 0 else \
                        (lambda e, bk=bk, g0=g0: e.activation(
                            u1.rearrange("p (c l) -> p c l", c=16)
                            [:, bk::4, :], banks8[g0 + bk]
                            .rearrange("p (s l) -> p s l", s=4),
                            AF.Identity))
                    E(eng, fn, waits=[("sPE", f1_done)] +
                      (u1_war if bk == 0 else []))
                    bank_free[g0 + bk] = [(sem, t.now(sem))]
                evac_sA = t.now("sA")
                bufs1 = [u1, pong1]
                for hi, hstep in enumerate((8, 4, 2, 1)):
                    iv = bufs1[hi % 2].rearrange(
                        "p (g two h l) -> p g two h l", two=2, h=hstep,
                        l=128)
                    ov = bufs1[(hi + 1) % 2].rearrange(
                        "p (g two h l) -> p g two h l", two=2, h=hstep,
                        l=128)
                    a = iv[:, :, 0, :, :]
                    bt = iv[:, :, 1, :, :]
                    w = [("sA", evac_sA)] + (pong1_war if hi == 0 else [])
                    E("vector", lambda e, a=a, bt=bt, ov=ov: e.tensor_add(
                        ov[:, :, 0, :, :], a, bt),
                        waits=w if hi == 0 else [])
                    E("vector", lambda e, a=a, bt=bt, ov=ov: e.tensor_tensor(
                        ov[:, :, 1, :, :], a, bt, ALU.subtract))
                pong1_war = []
                E("vector", lambda e: e.reduce_max(
                    mcol[:, 0:1], u1, mybir.AxisListType.X,
                    apply_absolute_value=True))
                E("vector", lambda e: e.tensor_scalar(
                    mcol[:, 0:1], mcol[:, 0:1], EPS, None, ALU.max))
                E("vector", lambda e: e.reciprocal(mcol[:, 1:2],
                                                   mcol[:, 0:1]))
                E("vector", lambda e: e.tensor_scalar(
                    mcol[:, 1:2], mcol[:, 1:2], 127.0, None, ALU.mult))
                mready = t.now("sV")
                E("scalar", lambda e: e.activation(
                    y1, u1, AF.Identity, scale=mcol[:, 1:2],
                    bias=self.csts[:, 2:3]),
                    waits=y1_war + [("sV", mready)])
                u1_war = [("sA", t.now("sA"))]
                ytok = E("scalar", lambda e: e.activation(
                    xq1t, y1, AF.Identity, bias=self.csts[:, 3:4]),
                    waits=xq1t_war)
                y1_war = [("sA", ytok)]
                E("vector", lambda e: e.tensor_mul(
                    mcol[:, 2:3], mcol[:, 0:1], self.sw_sb.ap()[:, 0:1]),
                    waits=[("sV", sw_ready)])
                E("vector", lambda e, tt=tt: e.tensor_mul(
                    self.alpha_sb[:, tt:tt + 1], mcol[:, 2:3], mcol[:, 2:3]))
                xtok = E("sync", lambda e, ti=ti: e.dma_start_transpose(
                    xqf[:, :, ti * 128:(ti + 1) * 128], xq1t),
                    waits=[("sA", ytok)] + xqf_war, inc="sX", inc_amt=16)
                xq1t_war = [("sX", xtok)]
                xq_toks[ti] = xtok
            xnr_war = [("sPE", t.now("sPE"))]

            for f2c in range(NF2C):
                b = f2c % 2
                for s_ in range(2):
                    ttok = E("sync", lambda e, f2c=f2c, b=b, s_=s_:
                             e.dma_start(
                        out=terns[b].rearrange("p (r s) f -> p r s f", r=8)
                        [:, :, s_, :],
                        in_=wT_up4[:, s_, :, f2c * 128:(f2c + 1) * 128]
                        .transpose([1, 0, 2])),
                        waits=(up_cache_wait + tern_war[b])
                        if s_ == 0 else (), inc=f"tu{b}", inc_amt=16)
                for sl in range(2):
                    bkid = (2 * f2c + sl) % 6
                    bank = banks6[bkid]
                    for dc in range(NDC):
                        w = []
                        if dc == 0:
                            w = [("sX", xq_toks[2 * sl + 1])] + \
                                bank_free[bkid]
                            if sl == 0:
                                w.append((f"tu{b}", ttok))
                        E("tensor", lambda e, dc=dc, b=b, bank=bank, sl=sl:
                            e.matmul(
                            bank[:, 0:256], terns[b][:, dc, :],
                            xqf[:, dc, sl * 256:(sl + 1) * 256],
                            start=(dc == 0), stop=(dc == NDC - 1)),
                            waits=w, inc="sPE")
                    sl_done = t.now("sPE")
                    atok = E("scalar", lambda e, f2c=f2c, bank=bank, sl=sl:
                             e.activation(
                        h2q[:, f2c, sl * 256:(sl + 1) * 256], bank[:, 0:256],
                        AF.Relu),
                        waits=[("sPE", sl_done)] +
                        (h2_war if (f2c == 0 and sl == 0) else []))
                    ptok = E("gpsimd", lambda e, f2c=f2c, sl=sl:
                             e.tensor_mul(
                        h2q[:, f2c, sl * 256:(sl + 1) * 256],
                        h2q[:, f2c, sl * 256:(sl + 1) * 256],
                        h2q[:, f2c, sl * 256:(sl + 1) * 256]),
                        waits=[("sA", atok)])
                    bank_free[bkid] = [("sP", ptok)]
                up_done = t.now("sPE")
                tern_war[b] = [("sPE", up_done)]
            h2_done = t.now("sP")

            for ti in range(4):
                tt = q * 4 + ti
                bu = ti % 2
                allfree = []
                for i in range(8):
                    allfree += bank_free[i]
                for c2 in range(NF2C):
                    w = [("sP", h2_done)]
                    if c2 == 0:
                        w += allfree
                    E("tensor", lambda e, c2=c2, ti=ti: e.matmul(
                        banks8[c2 % 8][:, (c2 // 8) * 128:
                                       (c2 // 8 + 1) * 128],
                        h2q[:, c2, ti * 128:(ti + 1) * 128], self.h128_sb[:],
                        start=True, stop=True),
                        waits=w, inc="sPE")
                f2_done = t.now("sPE")
                for bk in range(8):
                    eng = "vector" if bk % 2 == 0 else "scalar"
                    sem = "sV" if bk % 2 == 0 else "sA"
                    fn = (lambda e, bk=bk: e.tensor_copy(
                        u2.rearrange("p (c l) -> p c l", c=32)
                        [:, bk::8, :], banks8[bk]
                        .rearrange("p (s l) -> p s l", s=4))) \
                        if bk % 2 == 0 else \
                        (lambda e, bk=bk: e.activation(
                            u2.rearrange("p (c l) -> p c l", c=32)
                            [:, bk::8, :], banks8[bk]
                            .rearrange("p (s l) -> p s l", s=4),
                            AF.Identity))
                    E(eng, fn, waits=[("sPE", f2_done)] +
                      (u2_war if bk == 0 else []))
                    bank_free[bk] = [(sem, t.now(sem))]
                evac2_sA = t.now("sA")
                bufs2 = [u2, pong2]
                for hi, hstep in enumerate((16, 8, 4, 2, 1)):
                    iv = bufs2[hi % 2].rearrange(
                        "p (g two h l) -> p g two h l", two=2, h=hstep,
                        l=128)
                    ov = bufs2[(hi + 1) % 2].rearrange(
                        "p (g two h l) -> p g two h l", two=2, h=hstep,
                        l=128)
                    a = iv[:, :, 0, :, :]
                    bt = iv[:, :, 1, :, :]
                    w = [("sA", evac2_sA)] + \
                        (pong2_war + y2_war if hi == 0 else [])
                    E("vector", lambda e, a=a, bt=bt, ov=ov: e.tensor_add(
                        ov[:, :, 0, :, :], a, bt),
                        waits=w if hi == 0 else [])
                    E("vector", lambda e, a=a, bt=bt, ov=ov: e.tensor_tensor(
                        ov[:, :, 1, :, :], a, bt, ALU.subtract))
                pong2_war = []
                u2_war = [("sV", t.now("sV"))]
                E("vector", lambda e: e.reduce_max(
                    mcol[:, 3:4], pong2, mybir.AxisListType.X,
                    apply_absolute_value=True))
                E("vector", lambda e, tt=tt: e.tensor_mul(
                    mcol[:, 4:5], mcol[:, 3:4],
                    self.alpha_sb.ap()[:, tt:tt + 1]))
                E("vector", lambda e: e.tensor_scalar(
                    mcol[:, 4:5], mcol[:, 4:5], 1.0 / 64.0, EPS,
                    ALU.mult, ALU.max))
                E("vector", lambda e: e.reciprocal(mcol[:, 5:6],
                                                   mcol[:, 4:5]))
                E("vector", lambda e, tt=tt: e.tensor_mul(
                    mcol[:, 6:7], mcol[:, 5:6],
                    self.alpha_sb.ap()[:, tt:tt + 1]))
                E("vector", lambda e: e.tensor_scalar(
                    mcol[:, 6:7], mcol[:, 6:7], 127.0 / 64.0, None,
                    ALU.mult))
                btok = E("vector", lambda e: e.tensor_mul(
                    betac, mcol[:, 4:5], self.sw_sb.ap()[:, 1:2]),
                    waits=[("btw", t.now("btw"))])
                E("sync", lambda e, tt=tt: e.dma_start(
                    out=self.betaD[tt * 128:(tt + 1) * 128]
                    .rearrange("(p o) -> p o", o=1), in_=betac),
                    waits=[("sV", btok)], inc="btw", inc_amt=16)
                m2ready = t.now("sV")
                for hf in range(2):
                    E("scalar", lambda e, hf=hf: e.activation(
                        y2h, pong2[:, hf * 2048:(hf + 1) * 2048],
                        AF.Identity, scale=mcol[:, 6:7],
                        bias=self.csts[:, 2:3]),
                        waits=[("sV", m2ready)])
                    ytok2 = E("scalar", lambda e, hf=hf: e.activation(
                        xq2t[:, hf * 2048:(hf + 1) * 2048], y2h, AF.Identity,
                        bias=self.csts[:, 3:4]),
                        waits=xq2t_war if hf == 0 else [])
                    y2_war = [("sA", ytok2)]
                x2tok = E("sync", lambda e, ti=ti: e.dma_start_transpose(
                    xq2f[:, :, ti * 128:(ti + 1) * 128], xq2t),
                    waits=[("sA", t.now("sA"))] + xq2f_war, inc="sX",
                    inc_amt=16)
                xq2t_war = [("sX", x2tok)]
                x2_toks[ti] = x2tok
            h2_war = [("sPE", t.now("sPE"))]

            for dc in range(NDC):
                b = dc % 2
                for s_ in range(4):
                    ttok = E("sync", lambda e, dc=dc, b=b, s_=s_:
                             e.dma_start(
                        out=ternsD[b].rearrange("p (r s) f -> p r s f", r=8)
                        [:, :, s_, :],
                        in_=wT_dn4[:, s_, :, dc * 128:(dc + 1) * 128]
                        .transpose([1, 0, 2])),
                        waits=(dn_cache_wait + tern_war[b])
                        if s_ == 0 else (), inc=f"td{b}", inc_amt=16)
                rb = dc % 2
                rtoks = []
                for sl in range(2):
                    bkid = (2 * dc + sl) % 6
                    bank = banks6[bkid]
                    for c2 in range(NF2C):
                        w = []
                        if c2 == 0:
                            w = [("sX", x2_toks[2 * sl + 1])] + \
                                bank_free[bkid]
                            if sl == 0:
                                w.append((f"td{b}", ttok))
                        E("tensor", lambda e, c2=c2, b=b, bank=bank, sl=sl:
                            e.matmul(
                            bank[:, 0:256], ternsD[b][:, c2, :],
                            xq2f[:, c2, sl * 256:(sl + 1) * 256],
                            start=(c2 == 0), stop=(c2 == NF2C - 1)),
                            waits=w, inc="sPE")
                    sl_done = t.now("sPE")
                    rtok = E("vector", lambda e, rb=rb, bank=bank, sl=sl:
                             e.tensor_copy(
                        rawq[rb][:, sl * 256:(sl + 1) * 256], bank[:, 0:256]),
                        waits=[("sPE", sl_done)] +
                        (raw_war[rb] if sl == 0 else []))
                    rtoks.append(rtok)
                    bank_free[bkid] = [("sV", rtok)]
                dn_done = t.now("sPE")
                tern_war[b] = [("sPE", dn_done)]
                wtok = E("sync", lambda e, dc=dc, q=q, rb=rb: e.dma_start(
                    out=self.rawD[dc, :, q * 512:(q + 1) * 512],
                    in_=rawq[rb]),
                    waits=[("sV", rtoks[1])], inc=f"rdw{rb}", inc_amt=16)
                raw_war[rb] = [(f"rdw{rb}", wtok)]
            xq2f_war = [("sPE", t.now("sPE"))]
            xqf_war = [("sPE", t.now("sPE"))]
            h2_war = [("sPE", t.now("sPE")),
                      ("rdw0", t.now("rdw0")), ("rdw1", t.now("rdw1"))]

        self.rdw_toks = [("rdw0", t.now("rdw0")), ("rdw1", t.now("rdw1"))]
        self.btw_tok = ("btw", t.now("btw"))

    def s5(self):
        E, t = self.tape.emit, self.tape
        t.stage_barrier()
        R = self.scan1_R
        dmul = self.s2_dmul
        self.St0 = self.carve(self.A1, 29184, [128, 1024], BF16)
        Tk = [self.carve(self.A2, 36864, [128, 1024], F32),
              self.carve(self.A2, 40960, [128, 1024], F32)]
        Mk = self.carve(self.A2, 45056, [128, 1024], F32)
        mcols = self.carve(self.A2, 49152, [128, 16], F32)
        rhs_c = [self.carve(self.A2, 49280, [1, 32], F32),
                 self.carve(self.A2, 49408, [1, 32], F32)]
        ones64 = self.carve(self.A2, 49536, [1, 64], F32)

        cc = [("sCC", 1)]
        dtok = E("sync", lambda e: e.dma_start(
            out=self.dall_sb[0:1, 0:256],
            in_=self.ccoutD.rearrange("(r p) c -> r p c", p=128)
            [:, 0:1, 1024:1056]),
            waits=cc, inc="dal", inc_amt=16)
        E("vector", lambda e: e.memset(R, 0.0),
          waits=[("sV", t.now("sV"))])
        ones_tok = E("vector", lambda e: e.memset(ones64, 1.0))
        tk_war = {0: [], 1: []}
        rhs_war = {0: [], 1: []}
        lt_tok = {}
        pe_tok = {}
        pst_war = []
        mul_fn = {}

        def tk_load(k):
            kb = k % 2
            lt_tok[k] = E("sync", lambda e, k=k, kb=kb: e.dma_start(
                out=Tk[kb], in_=self.ccoutD[k * 128:(k + 1) * 128, 0:1024]),
                waits=cc + tk_war[kb], inc=f"tk{kb}", inc_amt=16)
        tk_load(0)
        tk_load(1)
        for k in range(8):
            kb = k % 2
            E("vector", lambda e, k=k: e.tensor_scalar(
                self.mr_sb[0:1, 0:32], self.dall_sb[0:1, 0:32],
                self.em_sb[0:1, k * 8 + 0:k * 8 + 1],
                self.om_sb[0:1, k * 8 + 0:k * 8 + 1],
                ALU.mult, ALU.add), waits=[("dal", dtok)] if k == 0 else [])
            for m in range(1, 8):
                E("vector", lambda e, k=k, m=m: e.tensor_scalar(
                    self.mr_sb[0:1, 32:64],
                    self.dall_sb[0:1, m * 32:(m + 1) * 32],
                    self.em_sb[0:1, k * 8 + m:k * 8 + m + 1],
                    self.om_sb[0:1, k * 8 + m:k * 8 + m + 1],
                    ALU.mult, ALU.add))
                E("vector", lambda e: e.tensor_mul(
                    self.mr_sb[0:1, 0:32], self.mr_sb[0:1, 0:32],
                    self.mr_sb[0:1, 32:64]))
            E("vector", lambda e, k=k: e.tensor_scalar(
                self.mr_sb[0:1, 0:32], self.mr_sb[0:1, 0:32],
                self.sel_sb[0:1, k:k + 1], None, ALU.mult))
            mv = self.mr_sb[0:1, 0:32].rearrange(
                "o (kh2 b2 j) -> o b2 kh2 j", kh2=4, b2=2)
            ctok = E("vector", lambda e, mv=mv, kb=kb: e.tensor_copy(
                rhs_c[kb].rearrange("o (b2 kh2 j) -> o b2 kh2 j",
                                    b2=2, kh2=4),
                mv), waits=rhs_war[kb])

            def mk_mul(k=k, kb=kb, ctok=ctok):
                nonlocal pst_war
                for b2 in range(2):
                    E("tensor", lambda e, b2=b2, kb=kb: e.matmul(
                        self.pst.ap()[b2 * 64:(b2 + 1) * 64, 0:16],
                        ones64, rhs_c[kb][0:1, b2 * 16:(b2 + 1) * 16],
                        start=True, stop=True,
                        tile_position=(0, 64) if b2 else (0, 0)),
                        waits=([("sV", ctok), ("sV", ones_tok)] + pst_war)
                        if b2 == 0 else [], inc="sPE")
                pe_tok[k] = t.now("sPE")
                rhs_war[kb] = [("sPE", pe_tok[k])]
                etok = E("vector", lambda e: e.tensor_copy(
                    mcols, self.pst.ap()[:, 0:16]),
                    waits=[("sPE", pe_tok[k])])
                pst_war = [("sV", etok)]
                mcv = mcols.unsqueeze(2).broadcast_to([128, 16, P])
                E("vector", lambda e, kb=kb, mcv=mcv: e.tensor_tensor(
                    Mk.rearrange("p (h pp) -> p h pp", h=16),
                    Tk[kb].rearrange("p (h pp) -> p h pp", h=16),
                    mcv, ALU.mult),
                    waits=[(f"tk{kb}", lt_tok[k])])
                multok = t.now("sV")
                E("vector", lambda e: e.tensor_add(R, R, Mk))
                tk_war[kb] = [("sV", multok)]
                if k + 2 < 8:
                    tk_load(k + 2)
            mul_fn[k] = mk_mul
            if k >= 1:
                mul_fn[k - 1]()
        mul_fn[7]()
        rin_done = t.now("sV")

        self._scan_state = dict(R=R, dmul=dmul, dm_war=self.s2_dm_war,
                                rin_done=rin_done)
        self.scan2_done = rin_done

    def St_slot(self, c):
        return self.St0 if c == 0 else self.staging[:, c - 1, :]

    def s6(self):
        E, t = self.tape.emit, self.tape
        t.stage_barrier()
        qbuf = self.s2_qbuf
        E("vector", lambda e: e.memset(qbuf[0], 0.0))
        qz_init6 = E("vector", lambda e: e.memset(qbuf[1], 0.0))
        self.z_fm = self.carve(self.A2, 0, [128, NDC, 2048], BF16)
        ysq = [self.carve(self.A1, 0, [128, 2048], F32),
               self.carve(self.A1, 8192, [128, 2048], F32)]
        y_full = [self.carve(self.A1, 16384, [128, 2048], F32),
                  self.carve(self.A1, 32768, [128, 2048], F32)]
        dmul2 = [self.carve(self.A1, 40960, [128, 1024], F32),
                 self.carve(self.A1, 45056, [128, 1024], F32)]
        z1 = self.carve(self.A1, 49152, [128, 2048], BF16)
        z1T = self.carve(self.A1, 53248, [128, 16, 128], BF16)
        sgg = self.carve(self.A1, 57344, [128, 16, 128], BF16)
        rms = self.carve(self.A1, 61440, [128, 32], F32)
        ydl = [self.carve(self.A3, 16384, [128, 2048], BF16),
               self.carve(self.A3, 20480, [128, 2048], BF16)]
        sg_war = []

        R = self._scan_state["R"]
        dm_war0 = self._scan_state["dm_war"]
        dm_war = {0: dm_war0[0] + dm_war0[1], 1: []}
        dm_tok = {}

        def dm_load(c):
            b = c % 2
            dm_tok[c] = E("sync", lambda e, b=b, c=c: e.dma_start(
                out=dmul2[b], in_=self.ddT[c]
                .rearrange("(b2 rest) -> b2 rest", b2=2)
                .unsqueeze(1).broadcast_to([2, 64, 1024])),
                waits=[self.ddT_ready] + dm_war[b], inc=f"dm6{b}",
                inc_amt=16)

        ydw_all = ("ydw", t.now("ydw"))
        yd_war = {0: [], 1: []}
        yd_tok = {}

        def yd_load(c):
            b = c % 2
            yd_tok[c] = E("sync", lambda e, b=b, c=c: e.dma_start(
                out=ydl[b], in_=self.ydD[c]),
                waits=[ydw_all] + yd_war[b], inc=f"yl{b}", inc_amt=16)

        qv = self.qD \
            .rearrange("c (two p) t -> (c two) p t", two=2) \
            .rearrange("(kh2 half j) p t -> half kh2 j p t",
                       kh2=4, half=2, j=4)
        q_tok = {}

        def q_load(c, extra):
            b = c % 2
            for hf in range(2):
                for kh2 in range(4):
                    h0 = kh2 * 8 + hf * 4
                    tok = E("sync", lambda e, b=b, c=c, hf=hf, kh2=kh2,
                            h0=h0: e.dma_start(
                        out=qbuf[b][hf * 64:(hf + 1) * 64, h0:h0 + 4, :],
                        in_=qv[hf, kh2, :, :, c * 128:(c + 1) * 128]
                        .transpose([1, 0, 2])),
                        waits=(self.qdw_toks + extra +
                               [("sV", qz_init6)])
                        if (hf == 0 and kh2 == 0) else (),
                        inc=f"q2{b}", inc_amt=16)
            q_tok[c] = tok

        dm_load(0)
        dm_load(1)
        yd_load(0)
        yd_load(1)
        q_load(0, [])
        q_load(1, [])

        ytt_hist = {-1: 0}
        z1tok_hist = {-2: 0, -1: 0}
        red_hist = {-2: 0, -1: 0}
        ptok_hist = {-2: 0, -1: 0}
        rtok_hist = {-1: 0}
        st_hist = {}
        ztok_prev = 0
        mtok_prev = 0
        pend_mtok = None

        def scan_step(c):
            b = c % 2
            dst = self.St0 if c == 0 else self.staging[:, c - 1, :]
            st_hist[c] = E("vector", lambda e, dst=dst: e.tensor_copy(
                dst, R))
            E("vector", lambda e, b=b: e.tensor_mul(R, R, dmul2[b]),
              waits=[(f"dm6{b}", dm_tok[c])])
            stok = E("vector", lambda e, c=c: e.tensor_add(
                R, R, self.staging[:, c, :]))
            dm_war[b] = [("sV", stok)]
            if c + 2 < NCH:
                dm_load(c + 2)

        scan_step(0)
        for c in range(NCH):
            b = c % 2
            for h in range(H):
                h2 = self.qslot(h)
                lhs = qbuf[b][:, h, :]
                stc = self.St_slot(c)
                rhs = stc[:, h2 * 64:h2 * 64 + 64]
                w = []
                if h == 0:
                    w = [(f"q2{b}", q_tok[c]), ("sV", st_hist[c]),
                         ("sV", ytt_hist[c - 1])]
                E("tensor", lambda e, h=h, lhs=lhs, rhs=rhs: e.matmul(
                    self.pb[h // 8].ap()[:, (h % 8) * 64:(h % 8) * 64 + 64],
                    lhs, rhs, start=True, stop=True),
                    waits=w, inc="sPE")
            mm4_done = t.now("sPE")
            if c + 2 < NCH:
                q_load(c + 2, [("sPE", mm4_done)])

            for bk in range(4):
                ev = self.stacked_tm.ap()[:, c, 64 + bk * 8:64 + bk * 8 + 8] \
                    .unsqueeze(2).broadcast_to([128, 8, P])
                E("vector", lambda e, bk=bk, ev=ev, b=b: e.tensor_mul(
                    y_full[b][:, bk * 512:(bk + 1) * 512]
                    .rearrange("p (h pp) -> p h pp", h=8),
                    self.pb[bk].ap().rearrange("p (h pp) -> p h pp", h=8),
                    ev), waits=[("sPE", mm4_done)] +
                    ([("sP", ptok_hist[c - 2])] if bk == 0 else []))
            ytt = t.now("sV")
            ytt_hist[c] = ytt
            padd = E("gpsimd", lambda e, b=b: e.tensor_add(
                y_full[b], y_full[b], ydl[b]),
                waits=[("sV", ytt), (f"yl{b}", yd_tok[c])])
            yd_war[b] = [("sP", padd)]
            if c + 2 < NCH:
                yd_load(c + 2)
            ptok = E("gpsimd", lambda e, b=b: e.tensor_mul(
                ysq[b], y_full[b], y_full[b]),
                waits=[("sV", red_hist[c - 2])])
            ptok_hist[c] = ptok
            if pend_mtok is not None:
                pend_mtok()
                pend_mtok = None
            if c + 1 < NCH:
                scan_step(c + 1)
            redt = E("vector", lambda e, b=b: e.tensor_reduce(
                rms, ysq[b].rearrange("p (h pp) -> p h pp", h=32),
                mybir.AxisListType.X, ALU.add),
                waits=[("sP", ptok), ("sA", rtok_hist[c - 1])])
            red_hist[c] = redt
            E("vector", lambda e: e.tensor_scalar(
                rms, rms, 1.0 / P, self.csts.ap()[:, 0:1], ALU.mult,
                ALU.add))
            vtok = t.now("sV")
            E("scalar", lambda e: e.activation(rms, rms, AF.Ln),
              waits=[("sV", vtok)])
            E("scalar", lambda e: e.activation(
                rms, rms, AF.Exp, scale=self.csts[:, 1:2]))
            rtok = t.now("sA")
            rtok_hist[c] = rtok
            rv = rms.unsqueeze(2).broadcast_to([128, 32, P])
            z1m = E("vector", lambda e, rv=rv, b=b: e.tensor_mul(
                z1.rearrange("p (h pp) -> p h pp", h=32),
                y_full[b].rearrange("p (h pp) -> p h pp", h=32), rv),
                waits=[("sA", rtok), ("sX", ztok_prev)])
            z1tok_hist[c] = z1m
            ztok = E("sync", lambda e: e.dma_start_transpose(z1T, z1),
                     waits=[("sV", z1m), ("sV", mtok_prev)], inc="sX",
                     inc_amt=16)
            ztok_prev = ztok
            for hv in range(2):
                sgtok = E("sync", lambda e, c=c, hv=hv: e.dma_start(
                    out=sgg[:, hv * 8:(hv + 1) * 8, :],
                    in_=self.sggD[hv * 8:(hv + 1) * 8, :,
                                  c * 128:(c + 1) * 128]
                    .transpose([1, 0, 2])),
                    waits=(self.sgw_toks + sg_war) if hv == 0 else (),
                    inc=f"sg{b}", inc_amt=16)

            def mk_mtok(c=c, b=b, ztok=ztok, sgtok=sgtok):
                nonlocal mtok_prev, sg_war
                mtok = E("vector", lambda e, c=c: e.tensor_mul(
                    self.z_fm[:, :, c * 128:(c + 1) * 128], z1T, sgg),
                    waits=[("sX", ztok), (f"sg{b}", sgtok)])
                mtok_prev = mtok
                sg_war = [("sV", mtok)]
            pend_mtok = mk_mtok
        pend_mtok()
        self.z_done = t.now("sV")

    def s7(self):
        E, t = self.tape.emit, self.tape
        t.stage_barrier()
        beta32 = self.carve(self.A3, 0, [128, 2048], F32)
        beta = self.carve(self.A3, 8192, [128, 2048], BF16)
        rawt = [self.carve(self.A3, 12288, [128, 512], BF16),
                self.carve(self.A3, 13312, [128, 512], BF16)]
        s1p = [self.carve(self.A3, 14336, [128, 512], BF16),
               self.carve(self.A3, 15360, [128, 512], BF16)]
        s1tg = self.carve(self.A1, 32768, [128, 4, 2048], BF16)
        xrl = [self.carve(self.A1, 49152, [128, 2048], F32),
               self.carve(self.A1, 57344, [128, 2048], F32)]
        outt = [self.carve(self.A1, 0, [128, 2048], F32),
                self.carve(self.A1, 8192, [128, 2048], F32)]

        wos = [self.carve(self.A4, 0, [128, NDC, 128], BF16),
               self.carve(self.A4, 4096, [128, NDC, 128], BF16)]
        wo4 = self.woT.rearrange("(dc p) f -> dc p f", p=128)
        wo_war = {0: [], 1: []}

        E("sync", lambda e: e.dma_start(
            out=beta32, in_=self.betaD.rearrange("(o a) -> o a", o=1)
            .broadcast_to([128, 2048])),
            waits=[self.btw_tok], inc="btr", inc_amt=16)
        btok = E("vector", lambda e: e.tensor_copy(beta, beta32),
                 waits=[("btr", 16), ("sV", self.z_done)])

        raw_war = {0: [], 1: []}
        s1_war = {0: [], 1: []}
        x_war = {0: [], 1: []}
        out_war = {0: [], 1: []}
        bank_free = {i: [] for i in range(6)}
        banks6 = [self.pb[i].ap() for i in range(6)]

        for tg in range(4):
            tptoks = []
            for fc in range(NDC):
                wb = fc % 2
                wtok = E("gpsimd", lambda e, fc=fc, wb=wb: e.dma_start(
                    out=wos[wb], in_=wo4[:, :, fc * 128:(fc + 1) * 128]
                    .transpose([1, 0, 2])),
                    waits=wo_war[wb], inc=f"wo{wb}", inc_amt=16)
                bkid = fc % 6
                bank = banks6[bkid]
                for dc in range(NDC):
                    w = []
                    if dc == 0:
                        w = [(f"wo{wb}", wtok), ("sV", self.z_done)] + \
                            bank_free[bkid]
                    E("tensor", lambda e, fc=fc, dc=dc, tg=tg, bank=bank,
                        wb=wb: e.matmul(
                        bank, wos[wb][:, dc, :],
                        self.z_fm[:, dc, tg * 512:(tg + 1) * 512],
                        start=(dc == 0), stop=(dc == NDC - 1)),
                        waits=w, inc="sPE")
                wo_war[wb] = [("sPE", t.now("sPE"))]
                wo_done = t.now("sPE")
                rb = fc % 2
                rtok = E("sync", lambda e, fc=fc, tg=tg, rb=rb: e.dma_start(
                    out=rawt[rb], in_=self.rawD[fc, :,
                                                tg * 512:(tg + 1) * 512]),
                    waits=self.rdw_toks + raw_war[rb], inc=f"rr{rb}",
                    inc_amt=16)
                E("vector", lambda e, rb=rb, tg=tg: e.tensor_mul(
                    s1p[rb], rawt[rb],
                    beta[:, tg * 512:(tg + 1) * 512]),
                    waits=[(f"rr{rb}", rtok), ("sV", btok)] + s1_war[rb])
                vtok = E("vector", lambda e, rb=rb, bank=bank: e.tensor_add(
                    s1p[rb], s1p[rb], bank), waits=[("sPE", wo_done)])
                raw_war[rb] = [("sV", vtok)]
                bank_free[bkid] = [("sV", vtok)]
                ttok = E("sync", lambda e, rb=rb, fc=fc: e.dma_start_transpose(
                    s1tg[:, :, fc * 128:(fc + 1) * 128], s1p[rb]),
                    waits=[("sV", vtok)], inc="sX", inc_amt=16)
                s1_war[rb] = [("sX", ttok)]
                tptoks.append(ttok)
            tp_done = tptoks[-1]
            for ti in range(4):
                tt = tg * 4 + ti
                ob = ti % 2
                xtok = E("sync", lambda e, tt=tt, ob=ob: e.dma_start(
                    out=xrl[ob], in_=self.x_in[1 + tt * 128:
                                               1 + (tt + 1) * 128, :]),
                    waits=x_war[ob], inc=f"xr2{ob}", inc_amt=16)
                otok = E("vector", lambda e, ti=ti, ob=ob: e.tensor_add(
                    outt[ob], s1tg[:, ti, :], xrl[ob]),
                    waits=[("sX", tp_done), (f"xr2{ob}", xtok)] +
                    out_war[ob])
                x_war[ob] = [("sV", otok)]
                ytok = E("sync", lambda e, tt=tt, ob=ob: e.dma_start(
                    out=self.y_out[tt * 128:(tt + 1) * 128, :],
                    in_=outt[ob]),
                    waits=[("sV", otok)], inc=f"yw{ob}", inc_amt=16)
                out_war[ob] = [(f"yw{ob}", ytok)]
            s1_war[0] = s1_war[0] + [("sV", t.now("sV"))]
            s1_war[1] = s1_war[1] + [("sV", t.now("sV"))]


    def drain_all(self):
        E, t = self.tape.emit, self.tape
        waits = [(n, t.now(n)) for n in t._names if t.now(n) > 0]
        def fin(e):
            return e.nop()
        E("sync", fin, waits=waits, dma=True)

    def finalize(self):
        nc, tape = self.nc, self.tape
        for s in tape._names:
            h = self.stack.enter_context(nc.semaphore(s))
            tape.handles[s] = h
        blk = self.stack.enter_context(nc.Block())

        def mk(eng_name):
            def body(eng):
                for fn in tape.items[eng_name]:
                    fn(eng)
            return body

        blk.sync(mk("sync"))
        blk.scalar(mk("scalar"))
        blk.tensor(mk("tensor"))
        blk.vector(mk("vector"))
        blk.gpsimd(mk("gpsimd"))
        self.stack.close()
        return nc


def host_inputs(full):
    x = np.asarray(full["L0_attn_hidden_states"], np.float32).reshape(16384, D)
    W = np.asarray(full["L0_attn_W_vkqgdt"], np.float32)
    conv_w = np.asarray(full["L0_attn_conv_w"], np.float32)
    conv_b = np.asarray(full["L0_attn_conv_b"], np.float32)
    dt_bias = np.asarray(full["L0_attn_dt_bias"], np.float32)
    alog = np.asarray(full["L0_attn_A_log_bias"], np.float32)
    gnorm = np.asarray(full["L0_attn_gnorm_w"], np.float32)
    Wo = np.asarray(full["L0_attn_W_o"], np.float32)
    wup = np.asarray(full["L0_ffn_w_up"], np.float32)
    wdn = np.asarray(full["L0_ffn_w_down"], np.float32)

    wT = np.ascontiguousarray(W.T)
    woT = np.ascontiguousarray(Wo.T)
    wupT = np.ascontiguousarray(wup.T)
    wdnT = np.ascontiguousarray(wdn.T)
    w0c = np.ascontiguousarray(conv_w[:, 0].reshape(CONVC, 128).T)
    w1c = np.ascontiguousarray(conv_w[:, 1].reshape(CONVC, 128).T)
    wbc = np.ascontiguousarray(conv_b.reshape(CONVC, 128).T)
    gn = np.tile(gnorm, 2).reshape(128, 1)
    dtb = dt_bias.reshape(H, 1)
    alogc = alog.reshape(H, 1)

    s_ = np.arange(128)[:, None]
    l_ = np.arange(128)[None, :]
    m = np.where(l_ < s_, np.float32(BIGNEG), np.float32(0.0))
    ltbigT = np.ascontiguousarray(np.tile(m, (1, H)))

    hm = np.ones((1, 1), np.float32)
    while hm.shape[0] < 128:
        hm = np.block([[hm, hm], [hm, -hm]])
    h128 = hm.astype(ml_dtypes.bfloat16)
    ident = np.eye(128, dtype=np.float32)
    identb = np.eye(128, dtype=ml_dtypes.bfloat16)

    shared = dict(wT=wT, woT=woT, w0c=w0c, w1c=w1c,
                  wbc=wbc, dtb=dtb, alog=alogc, gnormc=gn, ltbigT=ltbigT,
                  h128=h128, ident=ident, identb=identb)
    if _FAKE_CC:
        shared.update(wupT=wupT, wdnT=wdnT)

    maps = []
    for core in range(8):
        qtr = core % 4
        g0 = core * T
        x_sl = np.empty((T + 1, D), np.float32)
        x_sl[1:] = x[g0:g0 + T]
        x_sl[0] = 0.0 if qtr == 0 else x[g0 - 1]
        sel = np.zeros((1, 8), np.float32)
        em = np.zeros((8, 8), np.float32)
        grp = core // 4
        for k in range(grp * 4, grp * 4 + qtr):
            sel[0, k] = 1.0
            for mth in range(k + 1, grp * 4 + qtr):
                em[k, mth] = 1.0
        m_c = dict(shared, x_in=np.ascontiguousarray(x_sl),
                   sel8=sel, em8=em.reshape(1, 64),
                   om8=(1.0 - em).reshape(1, 64))
        if not _FAKE_CC:
            m_c["wupP"] = np.ascontiguousarray(
                wupT[core * 256:(core + 1) * 256, :])
            m_c["wdnP"] = np.ascontiguousarray(
                wdnT[core * 512:(core + 1) * 512, :])
        maps.append(m_c)
    return maps


def kernel(**inputs):
    mk = MK()
    mk.s0()
    mk.s1()
    mk.s15()
    mk.s2()
    mk.s4()
    mk.s5()
    mk.s6()
    mk.s7()
    mk.drain_all()
    nc = mk.finalize()
    maps = host_inputs(inputs)
    res = run_bass_kernel_spmd(nc, maps, core_ids=list(range(8)))
    out = np.concatenate([res.results[c]["y_out"] for c in range(8)], axis=0)
    return out.reshape(2, 8192, D).astype(np.float32)
